# version 02
# baseline (speedup 1.0000x reference)
import numpy as np
import ml_dtypes
import concourse.bass as bass
import concourse.mybir as mybir
from concourse.bass_utils import run_bass_kernel_spmd
from contextlib import ExitStack

F32 = mybir.dt.float32
BF16 = mybir.dt.bfloat16
AF = mybir.ActivationFunctionType
ALU = mybir.AluOpType
AX = mybir.AxisListType
BF = ml_dtypes.bfloat16

NCORES = 8
TLOC = 2048
D = 1024
NCH = D // 128
NB = TLOC // 512
GRAN = 2048
KB = 1024

ENGINES = ("sync", "scalar", "vector", "tensor", "gpsimd")


class Prog:
    def __init__(self, nc):
        self.nc = nc
        self.ops = []
        self.last_write = {}
        self.readers = {}

    def add(self, eng, emit, reads=(), writes=(), kind="compute", sticky=False,
            semkey=None):
        deps = set()
        for k in reads:
            w = self.last_write.get(k)
            if w is not None:
                deps.add(w)
        for k in writes:
            w = self.last_write.get(k)
            if w is not None:
                deps.add(w)
            for r in self.readers.get(k, ()):
                deps.add(r)
        i = len(self.ops)
        self.ops.append({"eng": eng, "emit": emit, "deps": deps, "kind": kind,
                         "sig": None, "need_sig": False, "sticky": sticky,
                         "semkey": semkey or (writes[0] if writes else None)})
        for k in reads:
            self.readers.setdefault(k, []).append(i)
        for k in writes:
            self.last_write[k] = i
            self.readers[k] = []
        return i

    def dma(self, eng, out, in_, reads=(), writes=(), **kw):
        assert writes, "dma needs write keys (semaphore keying)"
        semkey = kw.pop("semkey", None)
        return self.add(eng, lambda e: e.dma_start(out=out, in_=in_, **kw),
                        reads=reads, writes=writes, kind="dma", semkey=semkey)

    def cc(self, kind_, op, groups, ins, outs, reads=(), writes=()):
        return self.add(
            "gpsimd",
            lambda e: e.collective_compute(kind_, op, replica_groups=groups,
                                           ins=ins, outs=outs),
            reads=reads, writes=writes, kind="cc")

    def fence(self, eng="sync"):
        n = len(self.ops)
        i = self.add(eng, lambda e: e.nop(), kind="compute")
        self.ops[i]["deps"] = set(range(n))
        return i

    def build(self, block, ctx):
        nc = self.nc
        esem = {e: ctx.enter_context(nc.semaphore(f"e_{e}")) for e in ENGINES}
        dsem = {}
        dcnt = {}
        cc_i = 0
        for i, op in enumerate(self.ops):
            if op["kind"] == "dma":
                key = op["semkey"]
                if key not in dsem:
                    dsem[key] = ctx.enter_context(nc.semaphore(f"d{len(dsem)}"))
                    dcnt[key] = 0
                dcnt[key] += 16
                op["dsem"] = dsem[key]
                op["qval"] = dcnt[key]
            elif op["kind"] == "cc":
                op["dsem"] = ctx.enter_context(nc.semaphore(f"cc{cc_i}"))
                cc_i += 1
                op["qval"] = 1
        for op in self.ops:
            for d in op["deps"]:
                p = self.ops[d]
                if p["kind"] == "compute" and not (
                        p["eng"] == op["eng"] == "tensor"
                        and op["kind"] == "compute"
                        and not (op["sticky"] or p["sticky"])):
                    p["need_sig"] = True
        sig_cnt = {e: 0 for e in ENGINES}
        for op in self.ops:
            if op["need_sig"]:
                sig_cnt[op["eng"]] += 1
                op["sig"] = sig_cnt[op["eng"]]

        def emit_for(eng_name, eng):
            hw = {}
            for op in self.ops:
                if op["eng"] != eng_name:
                    continue
                waits = {}
                for d in op["deps"]:
                    p = self.ops[d]
                    if p["kind"] == "compute":
                        if (p["eng"] == eng_name == "tensor"
                                and op["kind"] == "compute"
                                and not (op["sticky"] or p["sticky"])):
                            continue
                        if p["sig"] is None:
                            continue
                        s, v = esem[p["eng"]], p["sig"]
                    else:
                        s, v = p["dsem"], p["qval"]
                    if v > waits.get(s, 0):
                        waits[s] = v
                for s, v in waits.items():
                    if v > hw.get(s, 0):
                        eng.wait_ge(s, v)
                        hw[s] = v
                inst = op["emit"](eng)
                if op["kind"] == "dma":
                    inst.then_inc(op["dsem"], 16)
                elif op["kind"] == "cc":
                    inst.then_inc(op["dsem"], 1)
                elif op["need_sig"]:
                    inst.then_inc(esem[eng_name], 1)

        @block.sync
        def _(sync):
            emit_for("sync", sync)

        @block.scalar
        def _(scalar):
            emit_for("scalar", scalar)

        @block.vector
        def _(vector):
            emit_for("vector", vector)

        @block.tensor
        def _(tensor):
            emit_for("tensor", tensor)

        @block.gpsimd
        def _(gpsimd):
            emit_for("gpsimd", gpsimd)


class Pool:

    def __init__(self, nc, ctx, name, nbytes):
        self.name = name
        self.nbytes = nbytes
        self.t = ctx.enter_context(nc.sbuf_tensor(name, [128, nbytes // 2], BF16))

    def view(self, off, shape, dtype=F32):
        esz = 4 if dtype == F32 else 2
        parts, free = shape
        assert off % esz == 0 and off + free * esz <= self.nbytes, (off, shape)
        ap = self.t[:].bitcast(dtype)[0:parts, off // esz: off // esz + free]
        keys = [f"{self.name}:{g}"
                for g in range(off // GRAN, (off + free * esz + GRAN - 1) // GRAN)]
        return ap, keys


class PsPool:
    def __init__(self, t, name):
        self.t = t
        self.name = name

    def view(self, off, shape, dtype=F32):
        esz = 4 if dtype == F32 else 2
        parts, free = shape
        ap = self.t[:].bitcast(dtype)[0:parts, off // esz: off // esz + free]
        keys = [f"{self.name}:{g}"
                for g in range(off // 2048, (off + free * esz + 2047) // 2048)]
        return ap, keys


class Ctx:
    pass


def blocks_oi(w):
    O, I = w.shape
    return np.ascontiguousarray(w.T.reshape(I, O // 128, 128).transpose(1, 0, 2))


def cols128(v):
    return np.ascontiguousarray(v.reshape(-1, 128).T)


def prep_weights(iw):
    w = {}
    w_in = iw["L0_attn_a_w_in"]
    b_in = iw["L0_attn_a_b_in"]
    vw = w_in.reshape(16, 192, 1024)[:, 128:192, :].reshape(1024, 1024)
    w["aa_v"] = blocks_oi(vw)
    w["aa_vb"] = cols128(b_in.reshape(16, 192)[:, 128:192].reshape(1024))
    w["aa_o"] = blocks_oi(iw["L0_attn_a_w_out"])
    w["aa_ob"] = cols128(iw["L0_attn_a_b_out"])
    w["f_w1"] = blocks_oi(iw["L1_ffn_w1_w"])
    w["f_w2"] = blocks_oi(iw["L1_ffn_w2_w"])
    w["f_h1"] = blocks_oi(iw["L1_ffn_h1_w"])
    w["f_h2"] = blocks_oi(iw["L1_ffn_h2_w"])
    w["f_w3"] = blocks_oi(iw["L1_ffn_w3_w"])
    w["f_s1"] = cols128(iw["L1_ffn_w1_s"])
    w["f_s2"] = cols128(iw["L1_ffn_w2_s"])
    w["f_sh1"] = cols128(iw["L1_ffn_h1_s"])
    w["f_sh2"] = cols128(iw["L1_ffn_h2_s"])
    w["f_s3"] = cols128(iw["L1_ffn_w3_s"])
    egT = iw["L0_ffn_eg_w"].transpose(0, 2, 1)
    euT = iw["L0_ffn_eu_w"].transpose(0, 2, 1)
    edT = iw["L0_ffn_ed_w"].transpose(0, 2, 1)
    sgT = iw["L0_ffn_sg_w"].T
    suT = iw["L0_ffn_su_w"].T
    sdT = iw["L0_ffn_sd_w"].T
    w["m_eg"] = np.ascontiguousarray(np.concatenate(
        [egT, sgT[None, :, 0:512], sgT[None, :, 512:1024]], axis=0))
    w["m_eu"] = np.ascontiguousarray(np.concatenate(
        [euT, suT[None, :, 0:512], suT[None, :, 512:1024]], axis=0))
    w["m_ed"] = np.ascontiguousarray(np.concatenate(
        [edT, sdT[None, 0:512, :], sdT[None, 512:1024, :]], axis=0))
    w["m_gate"] = np.ascontiguousarray(iw["L0_ffn_gate_w"].T)
    w["m_gbias"] = np.ascontiguousarray(
        iw["L0_ffn_gate_bias"].reshape(1, 64))
    w["ab_qkv"] = blocks_oi(iw["L0_attn_b_w_qkv"].T)
    w["ab_o"] = blocks_oi(iw["L0_attn_b_w_out"].T)
    w["ab_ob"] = cols128(iw["L0_attn_b_b_out"])
    w["wa_w1"] = blocks_oi(iw["L1_attn_a_W1"])
    w["wa_w2"] = blocks_oi(iw["L1_attn_a_W2"])
    w["wa_w3"] = blocks_oi(iw["L1_attn_a_W3"])
    wo_perm = iw["L1_attn_a_Wo"].reshape(1024, 128, 8).transpose(
        0, 2, 1).reshape(1024, 1024)
    w["wa_wopT"] = np.ascontiguousarray(wo_perm.T)
    invf = 1.0 / 10000.0 ** (np.arange(0, 128, 2, dtype=np.float64) / 128)
    fb = np.arange(8192) // 1024
    tt = np.arange(8192) % 1024
    pos = (tt * 8 + fb).astype(np.float64)
    ang = invf[:, None].repeat(2, 0).reshape(128)[:, None] * pos[None, :]
    f_d = np.repeat(invf, 2)
    ang = f_d[:, None] * pos[None, :]
    w["wa_cos"] = np.cos(ang).astype(np.float32)
    sgn = np.where(np.arange(128) % 2 == 0, -1.0, 1.0)
    w["wa_sin"] = (sgn[:, None] * np.sin(ang)).astype(np.float32)
    pfx = "L1_attn_b_"
    for nm, mat in (("wb_wr", iw[pfx + "Wr"]), ("wb_wk", iw[pfx + "Wk"]),
                    ("wb_wv", iw[pfx + "Wv"]), ("wb_wo", iw[pfx + "Wo"])):
        w[nm] = blocks_oi(np.ascontiguousarray(mat.T))
    w["wb_a1"] = np.ascontiguousarray(iw[pfx + "a1"])
    w["wb_a2"] = np.ascontiguousarray(iw[pfx + "a2"])
    w["wb_w1"] = np.ascontiguousarray(iw[pfx + "w1"])
    w["wb_w2"] = np.ascontiguousarray(iw[pfx + "w2"])
    w["wb_g1"] = np.ascontiguousarray(iw[pfx + "g1"])
    w["wb_g2"] = np.ascontiguousarray(iw[pfx + "g2"])
    for nm in ("x_r", "x_w", "x_k", "x_v", "x_a", "x_g", "w0", "a0",
               "k_k", "k_a"):
        w["wb_" + nm] = cols128(iw[pfx + nm].reshape(1024))
    w["wb_lnw"] = cols128(iw[pfx + "ln_w"])
    w["wb_lnb"] = cols128(iw[pfx + "ln_b"])
    rk = iw[pfx + "r_k"]
    rksel = np.zeros((8, 128, 2), np.float32)
    for j in range(8):
        rksel[j, 0:64, 0] = rk[2 * j]
        rksel[j, 64:128, 1] = rk[2 * j + 1]
    w["wb_rksel"] = rksel
    return w


WEIGHT_KEYS = {
    "s1": ["aa_v", "aa_vb", "aa_o", "aa_ob"],
    "s3": ["ab_qkv", "ab_o", "ab_ob"],
    "s5": ["wa_w1", "wa_w2", "wa_w3", "wa_wopT", "wa_cos", "wa_sin"],
    "s7a": ["wb_wr", "wb_wk", "wb_wv", "wb_a1", "wb_a2", "wb_w1", "wb_w2",
            "wb_g1", "wb_g2", "wb_x_r", "wb_x_w", "wb_x_k", "wb_x_v",
            "wb_x_a", "wb_x_g", "wb_w0", "wb_a0", "wb_k_k", "wb_k_a"],
    "s7b": [],
    "s7c": ["wb_wo", "wb_lnw", "wb_lnb", "wb_rksel"],
    "s2": ["m_eg", "m_eu", "m_ed", "m_gate", "m_gbias"],
    "s6": ["f_w1", "f_w2", "f_h1", "f_h2", "f_w3",
           "f_s1", "f_s2", "f_sh1", "f_sh2", "f_s3"],
}
WEIGHT_KEYS["s8"] = WEIGHT_KEYS["s6"]
WEIGHT_KEYS["s4"] = WEIGHT_KEYS["s2"]

SMALL_CONSTS = {
    "s1": ["aa_vb", "aa_ob"],
    "s3": ["ab_ob"],
    "s7a": ["wb_x_r", "wb_x_w", "wb_x_k", "wb_x_v", "wb_x_a", "wb_x_g",
            "wb_w0", "wb_a0", "wb_k_k", "wb_k_a"],
    "s7c": ["wb_lnw", "wb_lnb"],
    "s6": ["f_s1", "f_s2", "f_sh1", "f_sh2", "f_s3"],
}
SMALL_CONSTS["s8"] = SMALL_CONSTS["s6"]


def load_wblock(c, name, blk_idx, slot_off, nrows):
    p = c.prog
    nch_i = nrows // 128
    ap, keys = c.W.view(slot_off, (128, nch_i * 128), BF16)
    src = c.dram[name][blk_idx].rearrange("(i p) m -> p i m", p=128)
    dst = ap.rearrange("p (i m) -> p i m", i=nch_i)
    p.dma("gpsimd", dst, src, writes=keys)
    return ap, keys


def emit_ln(c):
    p = c.prog
    sum_ap, sum_k = c.P0.view(0, (1, TLOC), F32)
    ssq_ap, ssq_k = c.P1.view(0, (1, TLOC), F32)
    sq_ap, sq_k = c.S.view(0, (128, TLOC), F32)
    for j in range(NCH):
        xk = [f"x{j}"]
        p.add("scalar", lambda e, j=j: e.activation(sq_ap, c.x[j][:], AF.Square),
              reads=xk, writes=sq_k)
        for nb in range(NB):
            sl = slice(512 * nb, 512 * (nb + 1))
            p.add("tensor", lambda e, j=j, sl=sl: e.matmul(
                sum_ap[:, sl], c.ones_col[:], c.x[j][:, sl],
                start=(j == 0), stop=(j == NCH - 1)),
                reads=xk + ["ones_col"], writes=sum_k)
            p.add("tensor", lambda e, j=j, sl=sl: e.matmul(
                ssq_ap[:, sl], c.ones_col[:], sq_ap[:, sl],
                start=(j == 0), stop=(j == NCH - 1)),
                reads=sq_k + ["ones_col"], writes=ssq_k)
    mean_ap, mean_k = c.S.view(16 * KB, (1, TLOC), F32)
    var_ap, var_k = c.S.view(24 * KB, (1, TLOC), F32)
    msq_ap, msq_k = c.S.view(32 * KB, (1, TLOC), F32)
    p.add("vector", lambda e: e.tensor_scalar_mul(mean_ap, sum_ap, 1.0 / D),
          reads=sum_k, writes=mean_k)
    p.add("vector", lambda e: e.tensor_scalar_mul(var_ap, ssq_ap, 1.0 / D),
          reads=ssq_k, writes=var_k)
    p.add("vector", lambda e: e.tensor_mul(msq_ap, mean_ap, mean_ap),
          reads=mean_k, writes=msq_k)
    p.add("vector", lambda e: e.tensor_sub(var_ap, var_ap, msq_ap),
          reads=var_k + msq_k, writes=var_k)
    p.add("vector", lambda e: e.tensor_scalar_add(var_ap, var_ap, 1e-5),
          reads=var_k, writes=var_k)
    p.add("scalar", lambda e: e.activation(var_ap, var_ap, AF.Sqrt),
          reads=var_k, writes=var_k)
    p.add("vector", lambda e: e.reciprocal(var_ap, var_ap),
          reads=var_k, writes=var_k)
    rstd_ap, rstd_k = var_ap, var_k
    mB_ap, mB_k = c.S.view(0, (128, TLOC), F32)
    rB_ap, rB_k = c.S.view(8 * KB, (128, TLOC), F32)
    mb_ps, mb_psk = c.P0.view(0, (128, 512), F32)
    rb_ps, rb_psk = c.P1.view(0, (128, 512), F32)
    for nb in range(NB):
        sl = slice(512 * nb, 512 * (nb + 1))
        p.add("tensor", lambda e, sl=sl: e.matmul(
            mb_ps, c.ones_row[:], mean_ap[:, sl], start=True, stop=True),
            reads=mean_k + ["ones_row"] + sum_k, writes=mb_psk)
        p.add("vector", lambda e, sl=sl: e.tensor_copy(mB_ap[:, sl], mb_ps),
              reads=mb_psk, writes=mB_k)
        p.add("tensor", lambda e, sl=sl: e.matmul(
            rb_ps, c.ones_row[:], rstd_ap[:, sl], start=True, stop=True),
            reads=rstd_k + ["ones_row"] + ssq_k, writes=rb_psk)
        p.add("vector", lambda e, sl=sl: e.tensor_copy(rB_ap[:, sl], rb_ps),
              reads=rb_psk, writes=rB_k)
    tmp_ap, tmp_k = c.S.view(32 * KB, (128, TLOC), F32)
    for j in range(NCH):
        p.add("vector", lambda e, j=j: e.tensor_sub(tmp_ap, c.x[j][:], mB_ap),
              reads=[f"x{j}"] + mB_k, writes=tmp_k)
        p.add("vector", lambda e, j=j: e.tensor_mul(c.y[j][:], tmp_ap, rB_ap),
              reads=tmp_k + rB_k, writes=[f"y{j}"])


def emit_gemm_fm(c, wname, nch_in, nch_out, rhs_aps, rhs_keys, out_writer,
                 wslot_bytes, wslot0=0, nslots=2):
    p = c.prog
    nrows = nch_in * 128
    for j in range(nch_out):
        w_ap, w_k = load_wblock(c, wname, j, wslot0 + (j % nslots) * wslot_bytes,
                                nrows)
        ps_pool = c.P0 if j % 2 == 0 else c.P1
        ps_ap, ps_k = ps_pool.view(0, (128, TLOC), F32)
        for i in range(nch_in):
            for nb in range(NB):
                sl = slice(512 * nb, 512 * (nb + 1))
                p.add("tensor", lambda e, i=i, sl=sl, w_ap=w_ap, ps_ap=ps_ap,
                      st=(i == 0), sp=(i == nch_in - 1): e.matmul(
                          ps_ap[:, sl], w_ap[:, 128 * i:128 * (i + 1)],
                          rhs_aps[i][:, sl], start=st, stop=sp),
                    reads=w_k + rhs_keys[i], writes=ps_k)
        out_writer(j, ps_ap, ps_k)


def emit_l0_attn_a(c):
    p = c.prog
    v_aps, v_keys = [], []
    for j in range(NCH):
        ap, k = c.S.view(24 * KB + j * 4 * KB, (128, TLOC), BF16)
        v_aps.append(ap)
        v_keys.append(k)
    yv = [c.y[i][:] for i in range(NCH)]
    yk = [[f"y{i}"] for i in range(NCH)]

    def write_v(j, ps_ap, ps_k):
        p.add("vector", lambda e: e.tensor_scalar_add(
            v_aps[j], ps_ap, c.cn["aa_vb"][:, j:j + 1]),
            reads=ps_k + ["aa_vb"], writes=v_keys[j])

    emit_gemm_fm(c, "aa_v", NCH, NCH, yv, yk, write_v, 2 * KB, wslot0=0, nslots=2)

    def write_o(j, ps_ap, ps_k):
        p.add("vector", lambda e, j=j, ps_ap=ps_ap: e.scalar_tensor_tensor(
            c.x[j][:], ps_ap, c.cn["aa_ob"][:, j:j + 1], c.x[j][:],
            ALU.add, ALU.add),
            reads=ps_k + ["aa_ob", f"x{j}"], writes=[f"x{j}"])

    emit_gemm_fm(c, "aa_o", NCH, NCH, v_aps, v_keys, write_o, 2 * KB,
                 wslot0=4 * KB, nslots=2)


def emit_l1_ffn(c):
    p = c.prog
    NH = 32
    TBN = 4
    TB = TLOC // TBN

    def bank(pool, j):
        return pool.view(((j // 2) % 4) * 2048, (128, TB), F32)

    for tb in range(TBN):
        ts = slice(TB * tb, TB * (tb + 1))
        sA_ap, sA_k = c.S.view(0, (128, TB), F32)
        hA, hAk, hB, hBk = [], [], [], []
        for j in range(NH):
            a, k = c.S.view(2 * KB + j * TB * 2, (128, TB), BF16)
            hA.append(a)
            hAk.append(k)
            a, k = c.S.view(34 * KB + j * TB * 2, (128, TB), BF16)
            hB.append(a)
            hBk.append(k)

        for j in range(NH):
            w1_ap, w1_k = load_wblock(c, "f_w1", j, (j % 2) * 2 * KB, D)
            w2_ap, w2_k = load_wblock(c, "f_w2", j, 4 * KB + (j % 2) * 2 * KB, D)
            ps1_ap, ps1_k = bank(c.P0, j)
            ps2_ap, ps2_k = bank(c.P1, j)
            for i in range(NCH):
                p.add("tensor", lambda e, i=i, w=w1_ap, ps=ps1_ap, ts=ts,
                      st=(i == 0), sp=(i == NCH - 1): e.matmul(
                          ps, w[:, 128 * i:128 * (i + 1)], c.y[i][:, ts],
                          start=st, stop=sp),
                    reads=w1_k + [f"y{i}"], writes=ps1_k)
            for i in range(NCH):
                p.add("tensor", lambda e, i=i, w=w2_ap, ps=ps2_ap, ts=ts,
                      st=(i == 0), sp=(i == NCH - 1): e.matmul(
                          ps, w[:, 128 * i:128 * (i + 1)], c.y[i][:, ts],
                          start=st, stop=sp),
                    reads=w2_k + [f"y{i}"], writes=ps2_k)
            p.add("scalar", lambda e, j=j, ps=ps1_ap: e.activation(
                sA_ap, ps, AF.Silu, scale=c.cn["f_s1"][:, j:j + 1]),
                reads=ps1_k + ["f_s1"], writes=sA_k)
            p.add("vector", lambda e, j=j, ps=ps2_ap: e.scalar_tensor_tensor(
                hA[j], ps, c.cn["f_s2"][:, j:j + 1], sA_ap, ALU.mult, ALU.mult),
                reads=ps2_k + ["f_s2"] + sA_k, writes=hAk[j])

        def gemm(wname, nin, nout, src, srck, writer, slot0, slot_sz):
            for j in range(nout):
                w_ap, w_k = load_wblock(c, wname, j, slot0 + (j % 2) * slot_sz,
                                        nin * 128)
                ps_ap, ps_k = bank(c.P0 if j % 2 == 0 else c.P1, j)
                for i in range(nin):
                    p.add("tensor", lambda e, i=i, w=w_ap, ps=ps_ap,
                          st=(i == 0), sp=(i == nin - 1): e.matmul(
                              ps, w[:, 128 * i:128 * (i + 1)], src[i][:],
                              start=st, stop=sp),
                        reads=w_k + srck[i], writes=ps_k)
                writer(j, ps_ap, ps_k)

        def act_writer(dst, dstk, scale_name):
            def wr(j, ps_ap, ps_k):
                p.add("scalar", lambda e, j=j, ps_ap=ps_ap: e.activation(
                    dst[j], ps_ap, AF.Gelu, scale=c.cn[scale_name][:, j:j + 1]),
                    reads=ps_k + [scale_name], writes=dstk[j])
            return wr

        gemm("f_h1", NH, NH, hA, hAk, act_writer(hB, hBk, "f_sh1"), 0, 8 * KB)
        gemm("f_h2", NH, NH, hB, hBk, act_writer(hA, hAk, "f_sh2"), 0, 8 * KB)

        def w3_writer(j, ps_ap, ps_k):
            p.add("vector", lambda e, j=j, ps_ap=ps_ap, ts=ts:
                  e.scalar_tensor_tensor(
                      c.x[j][:, ts], ps_ap, c.cn["f_s3"][:, j:j + 1],
                      c.x[j][:, ts], ALU.mult, ALU.add),
                  reads=ps_k + ["f_s3", f"x{j}"], writes=[f"x{j}"])

        gemm("f_w3", NH, NCH, hA, hAk, w3_writer, 0, 8 * KB)


def load_wblock2(c, name, blk_idx, slot_off, nrows, mcols):
    p = c.prog
    nch = nrows // 128
    ap, keys = c.W.view(slot_off, (128, nch * mcols), BF16)
    src = c.dram[name][blk_idx].rearrange("(d p) m -> p d m", p=128)
    dst = ap.rearrange("p (d m) -> p d m", d=nch)
    p.dma("gpsimd", dst, src, writes=keys)
    return ap, keys


def emit_l0_ffn(c):
    p = c.prog
    nc = c.nc
    NT = TLOC // 128

    sc_aps, sc_keys = [], []
    for tc in range(NT):
        a, k = c.S.view(32 * KB + tc * 256, (128, 64), F32)
        sc_aps.append(a)
        sc_keys.append(k)
    gs_ap, gs_k = c.rt_gs[:], ["rt_gs"]
    m8_ap, m8_k = c.rt_m8[:], ["rt_m8"]
    gm_ap, gm_k = c.rt_gm[:], ["rt_gm"]
    msk_ap, msk_k = c.rt_msk[:], ["rt_msk"]
    cm_aps, cm_keys = [], []
    for tc in range(NT):
        a, k = c.S.view(38 * KB + tc * 256, (128, 64), F32)
        cm_aps.append(a)
        cm_keys.append(k)
    for tc in range(NT):
        tsl = slice(128 * tc, 128 * (tc + 1))
        gps_ap, gps_k = c.P0.view((tc % 4) * 2048, (128, 64), F32)
        for i in range(NCH):
            p.add("tensor", lambda e, i=i, tsl=tsl, ps=gps_ap,
                  st=(i == 0), sp=(i == NCH - 1): e.matmul(
                      ps, c.y[i][:, tsl], c.cn2["m_gate"][:, 64 * i:64 * (i + 1)],
                      start=st, stop=sp),
                reads=[f"y{i}", "m_gate"], writes=gps_k)
        p.add("scalar", lambda e, tc=tc, ps=gps_ap: e.activation(
            sc_aps[tc], ps, AF.Sigmoid), reads=gps_k, writes=sc_keys[tc])
        p.add("vector", lambda e, tc=tc: e.tensor_add(
            sc_aps[tc], sc_aps[tc], c.gb_bcast[:]),
            reads=sc_keys[tc] + ["gb_bcast"], writes=sc_keys[tc])
        p.add("vector", lambda e, tc=tc: e.reduce_max(
            gs_ap, sc_aps[tc].rearrange("p (g e) -> p g e", e=8), axis=AX.X),
            reads=sc_keys[tc], writes=gs_k)
        p.add("vector", lambda e: e.max(out=m8_ap, in_=gs_ap),
              reads=gs_k, writes=m8_k, sticky=True)
        if getattr(c, "tap", None) == "moe" and tc < 4:
            d8_ap, d8_k = c.S.view(73 * KB + tc * 64, (128, 16), F32)
            p.add("vector", lambda e, dbg=d8_ap: e.tensor_copy(
                dbg[:, 0:8], m8_ap), reads=m8_k, writes=d8_k)
            p.add("vector", lambda e, dbg=d8_ap: e.tensor_copy(
                dbg[:, 8:16], gs_ap), reads=gs_k, writes=d8_k)
            p.dma("gpsimd", c.d_tap[768:896, 1280 + 16 * tc:1280 + 16 * (tc + 1)],
                  d8_ap, reads=d8_k, writes=[f"tap8{tc}"])
        p.add("vector", lambda e: e.tensor_single_scalar(
            gm_ap, gs_ap, m8_ap[:, 3:4], ALU.is_ge),
            reads=gs_k + m8_k, writes=gm_k)
        for gidx in range(8):
            p.add("vector", lambda e, tc=tc, gidx=gidx: e.tensor_scalar_mul(
                msk_ap[:, 8 * gidx:8 * (gidx + 1)],
                sc_aps[tc][:, 8 * gidx:8 * (gidx + 1)],
                gm_ap[:, gidx:gidx + 1]),
                reads=sc_keys[tc] + gm_k, writes=msk_k)
        p.add("vector", lambda e: e.max(out=m8_ap, in_=msk_ap),
              reads=msk_k, writes=m8_k, sticky=True)
        p.add("vector", lambda e, tc=tc: e.tensor_single_scalar(
            cm_aps[tc], msk_ap, m8_ap[:, 5:6], ALU.is_ge),
            reads=msk_k + m8_k, writes=cm_keys[tc])
        if getattr(c, "tap", None) == "moe" and tc < 4:
            g3_ap, g3_k = c.S.view(70 * KB + 2 * KB + tc * 64, (128, 16), F32)
            p.add("vector", lambda e, tc=tc, dbg=g3_ap: e.tensor_copy(
                dbg[:, 0:8], gm_ap), reads=gm_k, writes=g3_k)
            p.add("vector", lambda e, tc=tc, dbg=g3_ap: e.tensor_copy(
                dbg[:, 8:16], m8_ap), reads=m8_k, writes=g3_k)
            p.dma("gpsimd", c.d_tap[768:896, 1024 + 16 * tc:1024 + 16 * (tc + 1)],
                  g3_ap, reads=g3_k, writes=[f"tapg{tc}"])
            dbg_ap, dbg_k = c.S.view(70 * KB + tc * 256, (128, 64), F32)
            p.add("vector", lambda e, tc=tc, dbg=dbg_ap: e.tensor_copy(dbg, msk_ap),
                  reads=msk_k, writes=dbg_k)
            p.dma("gpsimd", c.d_tap[768:896, 64 * tc:64 * (tc + 1)], dbg_ap,
                  reads=dbg_k, writes=[f"tapm{tc}"])
            dbg2_ap, dbg2_k = c.S.view(71 * KB + tc * 256, (128, 64), F32)
            p.add("vector", lambda e, tc=tc, dbg=dbg2_ap: e.tensor_copy(dbg, cm_aps[tc]),
                  reads=cm_keys[tc], writes=dbg2_k)
            p.dma("gpsimd", c.d_tap[896:1024, 64 * tc:64 * (tc + 1)], dbg2_ap,
                  reads=dbg2_k, writes=[f"tapsel{tc}"])
        p.add("vector", lambda e, tc=tc: e.tensor_mul(
            cm_aps[tc], cm_aps[tc], msk_ap),
            reads=cm_keys[tc] + msk_k, writes=cm_keys[tc])
        p.add("vector", lambda e, tc=tc: e.tensor_reduce(
            gs_ap[:, 0:1], cm_aps[tc], axis=AX.X, op=ALU.add),
            reads=cm_keys[tc], writes=gs_k)
        p.add("vector", lambda e: e.tensor_scalar_add(
            gs_ap[:, 0:1], gs_ap[:, 0:1], 1e-8), reads=gs_k, writes=gs_k)
        p.add("vector", lambda e: e.reciprocal(gs_ap[:, 0:1], gs_ap[:, 0:1]),
              reads=gs_k, writes=gs_k)
        p.add("vector", lambda e, tc=tc: e.tensor_scalar_mul(
            cm_aps[tc], cm_aps[tc], gs_ap[:, 0:1]),
            reads=cm_keys[tc] + gs_k, writes=cm_keys[tc])

    if getattr(c, "tap", None) == "moe":
        for tc in range(NT):
            p.dma("gpsimd", c.d_tap[0:128, 64 * tc:64 * (tc + 1)], cm_aps[tc],
                  reads=cm_keys[tc], writes=[f"tapc{tc}"])
            p.dma("gpsimd", c.d_tap[128:256, 64 * tc:64 * (tc + 1)], sc_aps[tc],
                  reads=sc_keys[tc], writes=[f"taps{tc}"])

    ctr_ap, ctr_k = c.S.view(42 * KB, (64, TLOC), BF16)
    for tc in range(NT):
        tp_ap, tp_k = c.P1.view((tc % 4) * 2048, (128, 128), F32)
        p.add("tensor", lambda e, tc=tc, ps=tp_ap: e.transpose(
            ps[0:64, :], cm_aps[tc], c.ident[:]),
            reads=cm_keys[tc] + ["ident"], writes=tp_k)
        p.add("vector", lambda e, tc=tc, ps=tp_ap: e.tensor_copy(
            ctr_ap[:, 128 * tc:128 * (tc + 1)], ps[0:64, :]),
            reads=tp_k, writes=ctr_k)
    p.dma("sync", c.dram_combrows, ctr_ap, reads=ctr_k, writes=["combrows"])

    hp = [[None] * 4, [None] * 4]
    hpk = [[None] * 4, [None] * 4]
    for g2 in range(2):
        for ii in range(4):
            a, k = c.S.view(g2 * 16 * KB + ii * 4 * KB, (128, TLOC), BF16)
            hp[g2][ii] = a
            hpk[g2][ii] = k
    crow_ap, crow_k = c.S.view(46 * KB, (1, TLOC), BF16)

    for g in range(33):
        ee = [2 * g, 2 * g + 1]
        for g2, e_ in enumerate(ee):
            is_shared = e_ >= 64
            eg_ap, eg_k = load_wblock2(c, "m_eg", e_, g2 * 8 * KB, D, 512)
            eu_ap, eu_k = load_wblock2(c, "m_eu", e_, 16 * KB + g2 * 8 * KB,
                                       D, 512)
            if not is_shared:
                p.dma("sync", crow_ap,
                      c.dram_combrows_r[e_:e_ + 1, :], reads=["combrows"],
                      writes=crow_k)
                cb_ap, cb_k = c.S.view(50 * KB + g2 * 4 * KB,
                                       (128, TLOC), BF16)
                cps_ap, cps_k = c.P1.view(g2 * 4096, (128, 2048), F32)                     if False else c.P1.view(0, (128, 2048), F32)
                for nb2 in range(NB):
                    sl = slice(512 * nb2, 512 * (nb2 + 1))
                    p.add("tensor", lambda e, sl=sl, ps=cps_ap: e.matmul(
                        ps[:, sl], c.ones_row_bf[:], crow_ap[:, sl],
                        start=True, stop=True),
                        reads=crow_k + ["ones_row_bf"], writes=cps_k)
                p.add("vector", lambda e, ps=cps_ap, cb=cb_ap: e.tensor_copy(
                    cb, ps), reads=cps_k, writes=cb_k)
            for ii in range(4):
                ps_ap, ps_k = c.P0.view(0, (128, 2048), F32) if ii % 2 == 0                     else c.P1.view(0, (128, 2048), F32)
                gt_ap, gt_k = c.S.view(58 * KB + (ii % 2) * 4 * KB,
                                       (128, TLOC), BF16)
                tm_ap, tm_k = c.S.view(66 * KB, (128, TLOC), BF16)
                for dd in range(NCH):
                    for nb2 in range(NB):
                        sl = slice(512 * nb2, 512 * (nb2 + 1))
                        p.add("tensor", lambda e, dd=dd, ii=ii, sl=sl,
                              w=eg_ap, ps=ps_ap,
                              st=(dd == 0), sp=(dd == NCH - 1): e.matmul(
                                  ps[:, sl], w[:, dd * 512 + 128 * ii:
                                               dd * 512 + 128 * (ii + 1)],
                                  c.y[dd][:, sl], start=st, stop=sp),
                            reads=eg_k + [f"y{dd}"], writes=ps_k)
                p.add("scalar", lambda e, ps=ps_ap, gt=gt_ap: e.activation(
                    gt, ps, AF.Silu), reads=ps_k, writes=gt_k)
                for dd in range(NCH):
                    for nb2 in range(NB):
                        sl = slice(512 * nb2, 512 * (nb2 + 1))
                        p.add("tensor", lambda e, dd=dd, ii=ii, sl=sl,
                              w=eu_ap, ps=ps_ap,
                              st=(dd == 0), sp=(dd == NCH - 1): e.matmul(
                                  ps[:, sl], w[:, dd * 512 + 128 * ii:
                                               dd * 512 + 128 * (ii + 1)],
                                  c.y[dd][:, sl], start=st, stop=sp),
                            reads=eu_k + [f"y{dd}"], writes=ps_k)
                if is_shared:
                    p.add("vector", lambda e, ps=ps_ap, gt=gt_ap,
                          hh=hp[g2][ii]: e.tensor_mul(hh, ps, gt),
                        reads=ps_k + gt_k, writes=hpk[g2][ii])
                else:
                    p.add("vector", lambda e, ps=ps_ap, gt=gt_ap, tm=tm_ap:
                          e.tensor_mul(tm, ps, gt),
                        reads=ps_k + gt_k, writes=tm_k)
                    p.add("vector", lambda e, tm=tm_ap, cb=cb_ap,
                          hh=hp[g2][ii]: e.tensor_mul(hh, tm, cb),
                        reads=tm_k + cb_k, writes=hpk[g2][ii])
        if getattr(c, "tap", None) == "moe" and g == 0:
            for g2 in range(2):
                for ii in range(4):
                    p.dma("scalar", c.d_tap[256 + (g2 * 4 + ii) * 64:
                                            256 + (g2 * 4 + ii) * 64 + 64,
                                            0:TLOC // 2],
                          hp[g2][ii][0:64, 0:TLOC // 2],
                          reads=hpk[g2][ii], writes=[f"taph{g2}{ii}"])
        ed_ap = [None, None]
        ed_k = [None, None]
        for g2, e_ in enumerate(ee):
            ed_ap[g2], ed_k[g2] = load_wblock2(c, "m_ed", e_, g2 * 8 * KB,
                                               512, D)
        for win in range(NB):
            sl = slice(512 * win, 512 * (win + 1))
            for j in range(NCH):
                pool = c.P0 if j < 4 else c.P1
                eps_ap, eps_k = pool.view((j % 4) * 2048, (128, 512), F32)
                first = True
                for g2 in range(2):
                    for ii in range(4):
                        p.add("tensor", lambda e, g2=g2, ii=ii, j=j, sl=sl,
                              ps=eps_ap, st=first, sp=(g2 == 1 and ii == 3):
                              e.matmul(
                                  ps, ed_ap[g2][:, ii * D + 128 * j:
                                                 ii * D + 128 * (j + 1)],
                                  hp[g2][ii][:, sl], start=st, stop=sp),
                            reads=ed_k[g2] + hpk[g2][ii], writes=eps_k)
                        first = False
                p.add("vector", lambda e, j=j, sl=sl, ps=eps_ap:
                      e.scalar_tensor_tensor(
                          c.x[j][:, sl], ps, 1.0, c.x[j][:, sl],
                          ALU.mult, ALU.add),
                    reads=eps_k + [f"x{j}"], writes=[f"x{j}"])


def emit_l0_attn_b(c):
    p = c.prog
    SC = 0.125
    grp = [[0, 1, 2, 3], [4, 5, 6, 7]]
    SEQ = 8192
    NKC = SEQ // 128

    qr, qrk = [], []
    for j in range(NCH):
        a, k = c.S.view(j * 2 * TLOC, (128, TLOC), BF16)
        qr.append(a)
        qrk.append(k)
    rot_ap, rot_k = c.S.view(32 * KB, (128, TLOC), BF16)
    tmpb_ap, tmpb_k = c.S.view(36 * KB, (128, TLOC), BF16)
    stg_ap, stg_k = c.S.view(40 * KB, (128, TLOC), BF16)

    def qkv_gemm(blk, ps_pool):
        w_ap, w_k = load_wblock(c, "ab_qkv", blk, (blk % 2) * 2 * KB, D)
        ps_ap, ps_k = ps_pool.view(0, (128, TLOC), F32)
        for i in range(NCH):
            for nb in range(NB):
                sl = slice(512 * nb, 512 * (nb + 1))
                p.add("tensor", lambda e, i=i, sl=sl, w=w_ap, ps=ps_ap,
                      st=(i == 0), sp=(i == NCH - 1): e.matmul(
                          ps[:, sl], w[:, 128 * i:128 * (i + 1)],
                          c.y[i][:, sl], start=st, stop=sp),
                    reads=w_k + [f"y{i}"], writes=ps_k)
        return ps_ap, ps_k

    def rope_apply(ps_ap, ps_k, dst_ap, dst_k, pm_pool):
        p.add("vector", lambda e, ps=ps_ap: e.tensor_copy(rot_ap, ps),
              reads=ps_k, writes=rot_k)
        pm_ap, pm_k = pm_pool.view(0, (128, TLOC), F32)
        for nb in range(NB):
            sl = slice(512 * nb, 512 * (nb + 1))
            p.add("tensor", lambda e, sl=sl, ps=pm_ap: e.matmul(
                ps[:, sl], c.permswap[:], rot_ap[:, sl], start=True, stop=True),
                reads=rot_k + ["permswap"], writes=pm_k)
        p.add("vector", lambda e, d=dst_ap: e.tensor_mul(d, rot_ap, c.ropec),
              reads=rot_k + c.ropec_k, writes=dst_k)
        p.add("vector", lambda e, ps=pm_ap: e.tensor_copy(tmpb_ap, ps),
              reads=pm_k, writes=tmpb_k)
        p.add("vector", lambda e: e.tensor_mul(tmpb_ap, tmpb_ap, c.ropes),
              reads=tmpb_k + c.ropes_k, writes=tmpb_k)
        p.add("vector", lambda e, d=dst_ap: e.tensor_add(d, d, tmpb_ap),
              reads=dst_k + tmpb_k, writes=dst_k)

    for j in range(NCH):
        ps_ap, ps_k = qkv_gemm(j, c.P0 if j % 2 == 0 else c.P1)
        rope_apply(ps_ap, ps_k, qr[j], qrk[j],
                   c.P1 if j % 2 == 0 else c.P0)
    for j in range(NCH):
        ps_ap, ps_k = qkv_gemm(8 + j, c.P0 if j % 2 == 0 else c.P1)
        rope_apply(ps_ap, ps_k, stg_ap, stg_k,
                   c.P1 if j % 2 == 0 else c.P0)
        p.dma("sync", c.cc_k_in[128 * j:128 * (j + 1), :], stg_ap,
              reads=stg_k, writes=[f"ccki{j}"], semkey="ccio")
    for j in range(NCH):
        ps_ap, ps_k = qkv_gemm(16 + j, c.P0 if j % 2 == 0 else c.P1)
        p.add("vector", lambda e, ps=ps_ap: e.tensor_copy(stg_ap, ps),
              reads=ps_k, writes=stg_k)
        p.dma("sync", c.ccvig[128 * j:128 * (j + 1), :], stg_ap,
              reads=stg_k, writes=[f"ccvi{j}"], semkey="ccio")

    p.cc("AllGather", ALU.bypass, grp, [c.cc_k_in.opt()], [c.cc_k_out.opt()],
         reads=[f"ccki{j}" for j in range(NCH)], writes=["cck"])
    p.cc("AllGather", ALU.bypass, grp, [c.ccvig.opt()], [c.ccvog.opt()],
         reads=[f"ccvi{j}" for j in range(NCH)], writes=["ccv"])

    if getattr(c, "tap", None) == "ab_qr":
        for j in range(NCH):
            p.dma("gpsimd", c.d_tap[128 * j:128 * (j + 1), :], qr[j],
                  reads=qrk[j], writes=[f"tq{j}"])
    sp_ap, sp_k = c.S.view(32 * KB, (128, SEQ), BF16)
    kh_ap, kh_k = c.S.view(48 * KB, (128, SEQ), BF16)
    vth_ap, vth_k = c.S.view(32 * KB, (128, SEQ), BF16)
    oh_ap, oh_k = c.S.view(80 * KB, (128, TLOC), BF16)
    nmx_ap, nmx_k = c.S.view(84 * KB, (128, 1), F32)
    nms_ap, nms_k = c.S.view(84 * KB + 64, (128, 1), F32)
    li_ap, li_k = c.S.view(84 * KB + 128, (128, 1), F32)
    vh_ap, vh_k = c.S.view(64 * KB, (128, 8192), BF16)

    NKC = 8192 // 128
    for hp in range(8):
        p.dma("sync", kh_ap.rearrange("p (r t) -> p r t", r=4),
              c.cc_k_out.rearrange("(r d) t -> d r t", d=1024)[
                  128 * hp:128 * (hp + 1)],
              reads=["cck"], writes=kh_k)
        p.dma("sync", vth_ap.rearrange("p (r t) -> p r t", r=4),
              c.ccvog.rearrange("(r d) t -> d r t", d=1024)[
                  128 * hp:128 * (hp + 1)],
              reads=["ccv"], writes=vth_k)
        for kc in range(NKC):
            tps_ap, tps_k = c.P1.view(4096 + (kc % 2) * 2048, (128, 128), BF16)
            p.add("tensor", lambda e, kc=kc, ps=tps_ap: e.transpose(
                ps, vth_ap[:, 128 * kc:128 * (kc + 1)], c.identbf[:]),
                reads=vth_k + ["identbf"], writes=tps_k)
            p.add("vector", lambda e, kc=kc, ps=tps_ap: e.tensor_copy(
                vh_ap[:, 128 * kc:128 * (kc + 1)], ps),
                reads=tps_k, writes=vh_k)
        if getattr(c, "tap", None) == "ab_kh" and hp == 0:
            p.dma("gpsimd", c.d_tap[0:128, :], kh_ap[:, 0:2048],
                  reads=kh_k, writes=["tkh"])
            p.dma("gpsimd", c.d_tap[128:256, :], vth_ap[:, 0:2048],
                  reads=vth_k, writes=["tvh"])
            p.dma("gpsimd", c.d_tap[256:384, :], vh_ap[:, 0:2048],
                  reads=vh_k, writes=["tvh2"])
        ops_ap, ops_k = c.P0.view(0, (128, TLOC), F32)
        for sub in range(2):
            h = 2 * hp + sub
            b0 = 64 * sub
            for qc in range(TLOC // 128):
                qsl = slice(128 * qc, 128 * (qc + 1))
                qh = qr[hp][b0:b0 + 64, qsl]
                for kb in range(8192 // 512):
                    sps_ap, sps_k = c.P1.view((kb % 2) * 2048, (128, 512), F32)
                    ksl = slice(512 * kb, 512 * (kb + 1))
                    p.add("tensor", lambda e, qh=qh, ksl=ksl, b0=b0,
                          ps=sps_ap: e.matmul(
                              ps, qh, kh_ap[b0:b0 + 64, ksl],
                              start=True, stop=True),
                        reads=qrk[hp] + kh_k, writes=sps_k)
                    p.add("vector", lambda e, ksl=ksl, ps=sps_ap: e.tensor_copy(
                        sp_ap[:, ksl], ps), reads=sps_k, writes=sp_k)
                if getattr(c, "tap", None) == "ab_sp" and h == 0 and qc == 0:
                    p.dma("gpsimd", c.d_tap[0:128, :], sp_ap[:, 0:2048],
                          reads=sp_k, writes=["tsp"])
                p.add("vector", lambda e: e.reduce_max(
                    nmx_ap, sp_ap, axis=AX.X, negate=True),
                    reads=sp_k, writes=nmx_k, sticky=True)
                p.add("vector", lambda e: e.tensor_scalar_mul(
                    nms_ap, nmx_ap, SC), reads=nmx_k, writes=nms_k)
                p.add("scalar", lambda e: e.activation(
                    sp_ap, sp_ap, AF.Exp, bias=nms_ap[:], scale=SC,
                    accum_out=li_ap[:]),
                    reads=sp_k + nms_k, writes=sp_k + li_k)
                p.add("vector", lambda e: e.reciprocal(li_ap, li_ap),
                      reads=li_k, writes=li_k)
                p.add("vector", lambda e: e.tensor_scalar_mul(
                    sp_ap, sp_ap, li_ap), reads=sp_k + li_k, writes=sp_k)
                for kc in range(NKC):
                    tps_ap, tps_k = c.P1.view(4096 + (kc % 2) * 2048,
                                              (128, 128), BF16)
                    pts_ap, pts_k = c.S.view(85 * KB + (kc % 2) * 256,
                                             (128, 128), BF16)
                    p.add("tensor", lambda e, kc=kc, ps=tps_ap: e.transpose(
                        ps, sp_ap[:, 128 * kc:128 * (kc + 1)], c.identbf[:]),
                        reads=sp_k + ["identbf"], writes=tps_k)
                    p.add("vector", lambda e, ps=tps_ap, pt=pts_ap:
                          e.tensor_copy(pt, ps), reads=tps_k, writes=pts_k)
                    p.add("tensor", lambda e, kc=kc, qsl=qsl, b0=b0,
                          pt=pts_ap, ps=ops_ap, st=(kc == 0),
                          sp2=(kc == NKC - 1): e.matmul(
                              ps[b0:b0 + 64, qsl],
                              vh_ap[:, 128 * kc + b0:128 * kc + b0 + 64], pt,
                              start=st, stop=sp2),
                        reads=pts_k + vh_k, writes=ops_k)
        p.add("vector", lambda e: e.tensor_copy(oh_ap, ops_ap),
              reads=ops_k, writes=oh_k)
        for j in range(NCH):
            wo_ap, wo_k = c.W.view(8 * KB + j * 2 * KB, (128, 8 * 128), BF16)
            lhs = wo_ap[:, 128 * hp:128 * (hp + 1)]
            ops2_ap, ops2_k = c.P1.view(0, (128, TLOC), F32)
            for nb in range(NB):
                sl = slice(512 * nb, 512 * (nb + 1))
                p.add("tensor", lambda e, lhs=lhs, sl=sl, ps=ops2_ap: e.matmul(
                    ps[:, sl], lhs, oh_ap[:, sl], start=True, stop=True),
                    reads=wo_k + oh_k, writes=ops2_k)
            p.add("vector", lambda e, j=j, ps=ops2_ap: e.scalar_tensor_tensor(
                c.x[j][:], ps, 1.0, c.x[j][:], ALU.mult, ALU.add),
                reads=ops2_k + [f"x{j}"], writes=[f"x{j}"])
    for j in range(NCH):
        p.add("vector", lambda e, j=j: e.tensor_scalar_add(
            c.x[j][:], c.x[j][:], c.cn["ab_ob"][:, j:j + 1]),
            reads=[f"x{j}", "ab_ob"], writes=[f"x{j}"])


def emit_l1_attn_a(c):
    p = c.prog
    SC5 = float(1.0 / np.sqrt(8.0))
    grp = [[0, 1, 2, 3], [4, 5, 6, 7]]
    SEQ = 8192
    NKC = SEQ // 128

    qB_ap, qB_k = c.S.view(0, (128, SEQ), BF16)
    kB_ap, kB_k = c.S.view(16 * KB, (128, SEQ), BF16)
    vB_ap, vB_k = c.S.view(32 * KB, (128, SEQ), BF16)
    cosT_ap, cosT_k = c.S.view(48 * KB, (128, SEQ), BF16)
    sinT_ap, sinT_k = c.S.view(64 * KB, (128, SEQ), BF16)
    stmp_ap, stmp_k = c.S.view(80 * KB, (128, TLOC), BF16)
    sp_ap, sp_k = c.S.view(48 * KB, (128, SEQ), BF16)
    nmx_ap, nmx_k = c.S.view(84 * KB, (128, 1), F32)
    nms_ap, nms_k = c.S.view(84 * KB + 64, (128, 1), F32)
    li_ap, li_k = c.S.view(84 * KB + 128, (128, 1), F32)
    pts_ap0, pts_k0 = c.S.view(84 * KB + 256, (128, 128), BF16)
    pts_ap1, pts_k1 = c.S.view(84 * KB + 512, (128, 128), BF16)
    stage_ap, stage_k = c.S.view(85 * KB, (128, 128), BF16)

    for hb in range(2):
        tblk = slice(1024 * hb, 1024 * (hb + 1))
        for name, dst_ap, dst_k in (("wa_w1", qB_ap, qB_k),
                                    ("wa_w2", kB_ap, kB_k),
                                    ("wa_w3", vB_ap, vB_k)):
            for fbch in range(NCH):
                w_ap, w_k = load_wblock(c, name, fbch, (fbch % 2) * 2 * KB, D)
                ps_ap, ps_k = (c.P0 if fbch % 2 == 0 else c.P1).view(
                    0, (128, 1024), F32)
                for i in range(NCH):
                    for nb2 in range(2):
                        sl_in = slice(1024 * hb + 512 * nb2,
                                      1024 * hb + 512 * (nb2 + 1))
                        sl_ps = slice(512 * nb2, 512 * (nb2 + 1))
                        p.add("tensor", lambda e, i=i, sl_in=sl_in,
                              sl_ps=sl_ps, w=w_ap, ps=ps_ap,
                              st=(i == 0), sp=(i == NCH - 1): e.matmul(
                                  ps[:, sl_ps], w[:, 128 * i:128 * (i + 1)],
                                  c.y[i][:, sl_in], start=st, stop=sp),
                            reads=w_k + [f"y{i}"], writes=ps_k)
                p.add("vector", lambda e, fbch=fbch, ps=ps_ap, d=dst_ap:
                      e.tensor_copy(d[:, 1024 * fbch:1024 * (fbch + 1)], ps),
                    reads=ps_k, writes=dst_k)
        p.dma("gpsimd", cosT_ap, c.dram["wa_cos"], writes=cosT_k)
        p.dma("gpsimd", sinT_ap, c.dram["wa_sin"], writes=sinT_k)
        for t_ap, t_k in ((qB_ap, qB_k), (kB_ap, kB_k)):
            for st4 in range(4):
                ssl = slice(2048 * st4, 2048 * (st4 + 1))
                pm_ap, pm_k = (c.P0 if st4 % 2 == 0 else c.P1).view(
                    0, (128, TLOC), F32)
                for nb2 in range(NB):
                    psl = slice(512 * nb2, 512 * (nb2 + 1))
                    gsl = slice(2048 * st4 + 512 * nb2,
                                2048 * st4 + 512 * (nb2 + 1))
                    p.add("tensor", lambda e, psl=psl, gsl=gsl, ps=pm_ap,
                          t=t_ap: e.matmul(
                              ps[:, psl], c.perm2[:], t[:, gsl],
                              start=True, stop=True),
                        reads=t_k + ["perm2"], writes=pm_k)
                p.add("vector", lambda e, ps=pm_ap: e.tensor_copy(stmp_ap, ps),
                      reads=pm_k, writes=stmp_k)
                p.add("vector", lambda e, ssl=ssl, t=t_ap: e.tensor_mul(
                    t[:, ssl], t[:, ssl], cosT_ap[:, ssl]),
                    reads=t_k + cosT_k, writes=t_k)
                p.add("vector", lambda e, ssl=ssl: e.tensor_mul(
                    stmp_ap, stmp_ap, sinT_ap[:, ssl]),
                    reads=stmp_k + sinT_k, writes=stmp_k)
                p.add("vector", lambda e, ssl=ssl, t=t_ap: e.tensor_add(
                    t[:, ssl], t[:, ssl], stmp_ap),
                    reads=t_k + stmp_k, writes=t_k)
        for kc in range(NKC):
            tps_ap, tps_k = c.P1.view(4096 + (kc % 2) * 2048, (128, 128), BF16)
            p.add("tensor", lambda e, kc=kc, ps=tps_ap: e.transpose(
                ps, vB_ap[:, 128 * kc:128 * (kc + 1)], c.identbf[:]),
                reads=vB_k + ["identbf"], writes=tps_k)
            p.add("vector", lambda e, kc=kc, ps=tps_ap: e.tensor_copy(
                vB_ap[:, 128 * kc:128 * (kc + 1)], ps),
                reads=tps_k, writes=vB_k)
        vh_ap, vh_k = vB_ap, vB_k
        for qc in range(NKC):
            qsl = slice(128 * qc, 128 * (qc + 1))
            for kb in range(SEQ // 512):
                sps_ap, sps_k = c.P1.view((kb % 2) * 2048, (128, 512), F32)
                ksl = slice(512 * kb, 512 * (kb + 1))
                p.add("tensor", lambda e, qsl=qsl, ksl=ksl, ps=sps_ap:
                      e.matmul(ps, qB_ap[:, qsl], kB_ap[:, ksl],
                               start=True, stop=True),
                    reads=qB_k + kB_k, writes=sps_k)
                p.add("vector", lambda e, ksl=ksl, ps=sps_ap: e.tensor_copy(
                    sp_ap[:, ksl], ps), reads=sps_k, writes=sp_k)
            p.add("vector", lambda e: e.reduce_max(
                nmx_ap, sp_ap, axis=AX.X, negate=True),
                reads=sp_k, writes=nmx_k, sticky=True)
            p.add("vector", lambda e: e.tensor_scalar_mul(nms_ap, nmx_ap, SC5),
                  reads=nmx_k, writes=nms_k)
            p.add("scalar", lambda e: e.activation(
                sp_ap, sp_ap, AF.Exp, bias=nms_ap[:], scale=SC5,
                accum_out=li_ap[:]),
                reads=sp_k + nms_k, writes=sp_k + li_k)
            p.add("vector", lambda e: e.reciprocal(li_ap, li_ap),
                  reads=li_k, writes=li_k)
            p.add("vector", lambda e: e.tensor_scalar_mul(sp_ap, sp_ap, li_ap),
                  reads=sp_k + li_k, writes=sp_k)
            ops_ap, ops_k = c.P0.view((qc % 2) * 2048, (128, 128), F32)
            for kc in range(NKC):
                tps_ap, tps_k = c.P1.view(4096 + (kc % 2) * 2048, (128, 128),
                                          BF16)
                pts_ap = pts_ap0 if kc % 2 == 0 else pts_ap1
                pts_k = pts_k0 if kc % 2 == 0 else pts_k1
                p.add("tensor", lambda e, kc=kc, ps=tps_ap: e.transpose(
                    ps, sp_ap[:, 128 * kc:128 * (kc + 1)], c.identbf[:]),
                    reads=sp_k + ["identbf"], writes=tps_k)
                p.add("vector", lambda e, ps=tps_ap, pt=pts_ap: e.tensor_copy(
                    pt, ps), reads=tps_k, writes=pts_k)
                p.add("tensor", lambda e, kc=kc, pt=pts_ap, ps=ops_ap,
                      st=(kc == 0), sp2=(kc == NKC - 1): e.matmul(
                          ps, pt, vh_ap[:, 128 * kc:128 * (kc + 1)],
                          start=st, stop=sp2),
                    reads=pts_k + vh_k, writes=ops_k)
            p.add("vector", lambda e, ps=ops_ap: e.tensor_copy(stage_ap, ps),
                  reads=ops_k, writes=stage_k)
            row = 8192 * hb + 128 * qc
            p.dma("sync", c.cc_a_in[row:row + 128, :],
                  stage_ap, reads=stage_k, writes=[f"ccai{hb}_{qc}"],
                  semkey="ccai")

    p.cc("AllGather", ALU.bypass, grp, [c.cc_a_in.opt()], [c.cc_a_out.opt()],
         reads=[f"ccai{hb}_{qc}" for hb in range(2) for qc in range(NKC)],
         writes=["cca"])

    for fbch in range(NCH):
        ap_, k_ = c.W.view(4 * KB + fbch * 2 * KB, (128, D), BF16)
        p.dma("gpsimd", ap_, c.dram["wa_wopT"][128 * fbch:128 * (fbch + 1), :],
              writes=k_)

    A_aps, A_keys = [], []
    for fbch in range(NCH):
        a_, k_ = c.S.view(48 * KB + fbch * 2 * KB, (128, D), BF16)
        A_aps.append(a_)
        A_keys.append(k_)
    for a in range(8):
        for fbch in range(NCH):
            for h in range(8):
                row = (h // 2) * 16384 + (h % 2) * 8192 + fbch * 1024 + 128 * a
                p.dma("gpsimd", A_aps[fbch][:, 128 * h:128 * (h + 1)],
                      c.cc_a_out[row:row + 128, :],
                      reads=["cca"], writes=A_keys[fbch], semkey="aread")
        for j2 in range(NCH):
            ps_ap, ps_k = (c.P0 if j2 % 2 == 0 else c.P1).view(
                0, (128, 1024), F32)
            for fbch in range(NCH):
                wo_ap, wo_k = c.W.view(4 * KB + fbch * 2 * KB, (128, D), BF16)
                for nb2 in range(2):
                    psl = slice(512 * nb2, 512 * (nb2 + 1))
                    p.add("tensor", lambda e, fbch=fbch, j2=j2, psl=psl,
                          w=wo_ap, ps=ps_ap, st=(fbch == 0),
                          sp=(fbch == NCH - 1): e.matmul(
                              ps[:, psl],
                              A_aps[fbch][:, 128 * j2:128 * (j2 + 1)],
                              w[:, psl], start=st, stop=sp),
                        reads=A_keys[fbch] + wo_k, writes=ps_k)
            p.add("vector", lambda e, j2=j2, a=a, ps=ps_ap:
                  e.scalar_tensor_tensor(
                      c.x[j2][:, 1024 * (a % 2):1024 * (a % 2 + 1)], ps,
                      c.wamask[:, a:a + 1],
                      c.x[j2][:, 1024 * (a % 2):1024 * (a % 2 + 1)],
                      ALU.mult, ALU.add),
                reads=ps_k + [f"x{j2}", "wamask"], writes=[f"x{j2}"])


def emit_s7a(c):
    p = c.prog
    nc = c.nc
    outs = {}
    for nm in ("o_r", "o_lw", "o_k2", "o_v", "o_kk", "o_b", "o_g"):
        outs[nm] = nc.dram_tensor(nm, [D, TLOC], BF16, kind="ExternalOutput").ap()
    d_xprev = c.dram["xprev"]

    D_aps, D_keys, M_aps, M_keys = [], [], [], []
    for j in range(NCH):
        a, k = c.S.view(j * 4 * KB, (128, TLOC), BF16)
        D_aps.append(a)
        D_keys.append(k)
        a, k = c.S.view(32 * KB + j * 4 * KB, (128, TLOC), BF16)
        M_aps.append(a)
        M_keys.append(k)
    t1a_ap, t1a_k = c.S.view(64 * KB, (64, TLOC), BF16)
    t1w_ap, t1w_k = c.S.view(68 * KB, (64, TLOC), BF16)
    kch_ap, kch_k = c.S.view(72 * KB, (128, TLOC), BF16)
    scr_ap, scr_k = c.S.view(76 * KB, (128, TLOC), BF16)
    sc2_ap, sc2_k = c.S.view(80 * KB, (128, TLOC), BF16)
    xp_sb = [None] * NCH
    xp_k = [None] * NCH
    for j in range(NCH):
        xp_sb[j], xp_k[j] = c.S.view(92 * KB + j * 8, (128, 1), F32)
    yp_sb, yp_kk = [], []
    for j in range(NCH):
        a, k = c.S.view(92 * KB + 256 + j * 8, (128, 1), F32)
        yp_sb.append(a)
        yp_kk.append(k)
    st_ap, st_k = c.S.view(92 * KB + 512, (128, 8), F32)
    n2_ap, n2_k = c.S.view(84 * KB, (2, TLOC), F32)

    for j in range(NCH):
        p.dma("sync", xp_sb[j], d_xprev[128 * j:128 * (j + 1), :],
              writes=xp_k[j], semkey="xprev")
    ps_s, ps_sk = c.P0.view(0, (1, 1), F32)
    ps_q, ps_qk = c.P1.view(0, (1, 1), F32)
    for j in range(NCH):
        p.add("scalar", lambda e, j=j: e.activation(
            st_ap[:, 0:1], xp_sb[j], AF.Square), reads=xp_k[j], writes=st_k)
        p.add("tensor", lambda e, j=j: e.matmul(
            ps_s, c.ones_col[:], xp_sb[j], start=(j == 0), stop=(j == NCH - 1)),
            reads=xp_k[j] + ["ones_col"], writes=ps_sk)
        p.add("tensor", lambda e, j=j: e.matmul(
            ps_q, c.ones_col[:], st_ap[:, 0:1], start=(j == 0),
            stop=(j == NCH - 1)),
            reads=st_k + ["ones_col"], writes=ps_qk)
    mu_ap, mu_k = c.S.view(92 * KB + 1536, (1, 4), F32)
    p.add("vector", lambda e: e.tensor_scalar_mul(mu_ap[:, 0:1], ps_s, 1.0 / D),
          reads=ps_sk, writes=mu_k)
    p.add("vector", lambda e: e.tensor_scalar_mul(mu_ap[:, 1:2], ps_q, 1.0 / D),
          reads=ps_qk, writes=mu_k)
    p.add("vector", lambda e: e.tensor_mul(mu_ap[:, 2:3], mu_ap[:, 0:1],
                                           mu_ap[:, 0:1]),
          reads=mu_k, writes=mu_k)
    p.add("vector", lambda e: e.tensor_sub(mu_ap[:, 1:2], mu_ap[:, 1:2],
                                           mu_ap[:, 2:3]),
          reads=mu_k, writes=mu_k)
    p.add("vector", lambda e: e.tensor_scalar_add(mu_ap[:, 1:2], mu_ap[:, 1:2],
                                                  1e-5),
          reads=mu_k, writes=mu_k)
    p.add("scalar", lambda e: e.activation(mu_ap[:, 1:2], mu_ap[:, 1:2],
                                           AF.Sqrt),
          reads=mu_k, writes=mu_k)
    p.add("vector", lambda e: e.reciprocal(mu_ap[:, 1:2], mu_ap[:, 1:2]),
          reads=mu_k, writes=mu_k)
    mb_ps, mb_psk = c.P0.view(0, (128, 2), F32)
    p.add("tensor", lambda e: e.matmul(mb_ps, c.ones_row[:], mu_ap[:, 0:2],
                                       start=True, stop=True),
          reads=mu_k + ["ones_row"] + ps_sk, writes=mb_psk)
    for j in range(NCH):
        p.add("vector", lambda e, j=j: e.tensor_sub(
            yp_sb[j], xp_sb[j], mb_ps[:, 0:1]),
            reads=xp_k[j] + mb_psk, writes=yp_kk[j])
        p.add("vector", lambda e, j=j: e.tensor_mul(
            yp_sb[j], yp_sb[j], mb_ps[:, 1:2]),
            reads=yp_kk[j] + mb_psk, writes=yp_kk[j])

    for j in range(NCH):
        p.add("vector", lambda e, j=j: e.tensor_sub(
            D_aps[j][:, 1:TLOC], c.y[j][:, 0:TLOC - 1], c.y[j][:, 1:TLOC]),
            reads=[f"y{j}"], writes=D_keys[j])
        p.add("vector", lambda e, j=j: e.tensor_sub(
            D_aps[j][:, 0:1], yp_sb[j], c.y[j][:, 0:1]),
            reads=yp_kk[j] + [f"y{j}"], writes=D_keys[j])

    def mk_mix(coef):
        for j in range(NCH):
            p.add("vector", lambda e, j=j: e.scalar_tensor_tensor(
                M_aps[j], D_aps[j], c.cn[coef][:, j:j + 1], c.y[j][:],
                ALU.mult, ALU.add),
                reads=D_keys[j] + [coef, f"y{j}"], writes=M_keys[j])

    def small_in(name, shape):
        t = c.sb_consts[name]
        return t

    mk_mix("wb_x_a")
    psa_ap, psa_k = c.P0.view(0, (64, TLOC), F32)
    for i in range(NCH):
        for nb in range(NB):
            sl = slice(512 * nb, 512 * (nb + 1))
            p.add("tensor", lambda e, i=i, sl=sl: e.matmul(
                psa_ap[:, sl], c.sb_a1[:, 64 * i:64 * (i + 1)], M_aps[i][:, sl],
                start=(i == 0), stop=(i == NCH - 1)),
                reads=["sb_a1"] + M_keys[i], writes=psa_k)
    p.add("vector", lambda e: e.tensor_copy(t1a_ap, psa_ap),
          reads=psa_k, writes=t1a_k)

    mk_mix("wb_x_w")
    psw_ap, psw_k = c.P1.view(0, (64, TLOC), F32)
    for i in range(NCH):
        for nb in range(NB):
            sl = slice(512 * nb, 512 * (nb + 1))
            p.add("tensor", lambda e, i=i, sl=sl: e.matmul(
                psw_ap[:, sl], c.sb_w1[:, 64 * i:64 * (i + 1)], M_aps[i][:, sl],
                start=(i == 0), stop=(i == NCH - 1)),
                reads=["sb_w1"] + M_keys[i], writes=psw_k)
    p.add("scalar", lambda e: e.activation(t1w_ap, psw_ap, AF.Tanh),
          reads=psw_k, writes=t1w_k)
    for j in range(NCH):
        psz_ap, psz_k = (c.P0 if j % 2 == 0 else c.P1).view(0, (128, TLOC), F32)
        for nb in range(NB):
            sl = slice(512 * nb, 512 * (nb + 1))
            p.add("tensor", lambda e, j=j, sl=sl, ps=psz_ap: e.matmul(
                ps[:, sl], c.sb_w2[:, 128 * j:128 * (j + 1)], t1w_ap[:, sl],
                start=True, stop=True),
                reads=["sb_w2"] + t1w_k, writes=psz_k)
        p.add("vector", lambda e, j=j, ps=psz_ap: e.tensor_scalar_add(
            scr_ap, ps, c.cn["wb_w0"][:, j:j + 1]),
            reads=psz_k + ["wb_w0"], writes=scr_k)
        p.add("scalar", lambda e: e.activation(sc2_ap, scr_ap, AF.Abs),
              reads=scr_k, writes=sc2_k)
        p.add("scalar", lambda e: e.activation(sc2_ap, sc2_ap, AF.Exp,
                                               scale=-1.0),
              reads=sc2_k, writes=sc2_k)
        p.add("vector", lambda e: e.tensor_scalar_add(sc2_ap, sc2_ap, 1.0),
              reads=sc2_k, writes=sc2_k)
        p.add("scalar", lambda e: e.activation(sc2_ap, sc2_ap, AF.Ln),
              reads=sc2_k, writes=sc2_k)
        p.add("vector", lambda e: e.tensor_scalar_min(scr_ap, scr_ap, 0.0),
              reads=scr_k, writes=scr_k)
        p.add("vector", lambda e: e.tensor_sub(scr_ap, scr_ap, sc2_ap),
              reads=scr_k + sc2_k, writes=scr_k)
        p.add("vector", lambda e: e.tensor_scalar_add(scr_ap, scr_ap, -0.5),
              reads=scr_k, writes=scr_k)
        p.add("scalar", lambda e: e.activation(scr_ap, scr_ap, AF.Exp),
              reads=scr_k, writes=scr_k)
        p.add("vector", lambda e: e.tensor_scalar_mul(scr_ap, scr_ap, -1.0),
              reads=scr_k, writes=scr_k)
        p.dma("sync", outs["o_lw"][128 * j:128 * (j + 1), :], scr_ap,
              reads=scr_k, writes=[f"olw{j}"], semkey="o_scr")

    for mixc, wname, oname in (("wb_x_r", "wb_wr", "o_r"),
                               ("wb_x_v", "wb_wv", "o_v")):
        mk_mix(mixc)
        for j in range(NCH):
            w_ap, w_k = load_wblock(c, wname, j, (j % 2) * 2 * KB, D)
            ps_ap, ps_k = (c.P0 if j % 2 == 0 else c.P1).view(
                0, (128, TLOC), F32)
            for i in range(NCH):
                for nb in range(NB):
                    sl = slice(512 * nb, 512 * (nb + 1))
                    p.add("tensor", lambda e, i=i, sl=sl, w=w_ap, ps=ps_ap,
                          st=(i == 0), sp=(i == NCH - 1): e.matmul(
                              ps[:, sl], w[:, 128 * i:128 * (i + 1)],
                              M_aps[i][:, sl], start=st, stop=sp),
                        reads=w_k + M_keys[i], writes=ps_k)
            p.add("vector", lambda e, ps=ps_ap: e.tensor_copy(scr_ap, ps),
                  reads=ps_k, writes=scr_k)
            p.dma("sync", outs[oname][128 * j:128 * (j + 1), :], scr_ap,
                  reads=scr_k, writes=[f"{oname}{j}"], semkey="o_scr")

    mk_mix("wb_x_k")
    omka_ap, omka_k = c.S.view(92 * KB + 1600, (128, 8), F32)
    p.add("vector", lambda e: e.tensor_scalar(
        omka_ap, c.cn["wb_k_a"][:], -1.0, 1.0, ALU.mult, ALU.add),
        reads=["wb_k_a"], writes=omka_k)
    for j in range(NCH):
        w_ap, w_k = load_wblock(c, "wb_wk", j, (j % 2) * 2 * KB, D)
        ps_ap, ps_k = (c.P0 if j % 2 == 0 else c.P1).view(0, (128, TLOC), F32)
        for i in range(NCH):
            for nb in range(NB):
                sl = slice(512 * nb, 512 * (nb + 1))
                p.add("tensor", lambda e, i=i, sl=sl, w=w_ap, ps=ps_ap,
                      st=(i == 0), sp=(i == NCH - 1): e.matmul(
                          ps[:, sl], w[:, 128 * i:128 * (i + 1)],
                          M_aps[i][:, sl], start=st, stop=sp),
                    reads=w_k + M_keys[i], writes=ps_k)
        p.add("vector", lambda e, ps=ps_ap: e.tensor_copy(kch_ap, ps),
              reads=ps_k, writes=kch_k)
        psj_ap, psj_k = (c.P1 if j % 2 == 0 else c.P0).view(0, (128, TLOC), F32)
        for nb in range(NB):
            sl = slice(512 * nb, 512 * (nb + 1))
            p.add("tensor", lambda e, j=j, sl=sl, ps=psj_ap: e.matmul(
                ps[:, sl], c.sb_a2[:, 128 * j:128 * (j + 1)], t1a_ap[:, sl],
                start=True, stop=True),
                reads=["sb_a2"] + t1a_k, writes=psj_k)
        aj_ap, aj_k = c.S.view(68 * KB, (128, TLOC), BF16)
        p.add("scalar", lambda e, j=j, ps=psj_ap: e.activation(
            aj_ap, ps, AF.Sigmoid, bias=c.cn["wb_a0"][:, j:j + 1]),
            reads=psj_k + ["wb_a0"], writes=aj_k)
        p.add("vector", lambda e, j=j: e.tensor_scalar_mul(
            scr_ap, kch_ap, c.cn["wb_k_k"][:, j:j + 1]),
            reads=kch_k + ["wb_k_k"], writes=scr_k)
        p.add("scalar", lambda e: e.activation(sc2_ap, scr_ap, AF.Square),
              reads=scr_k, writes=sc2_k)
        nps_ap, nps_k = c.P0.view(0, (2, TLOC), F32)
        for nb in range(NB):
            sl = slice(512 * nb, 512 * (nb + 1))
            p.add("tensor", lambda e, sl=sl: e.matmul(
                nps_ap[:, sl], c.hs2[:], sc2_ap[:, sl], start=True, stop=True),
                reads=sc2_k + ["hs2"], writes=nps_k)
        p.add("scalar", lambda e: e.activation(n2_ap, nps_ap, AF.Sqrt),
              reads=nps_k, writes=n2_k)
        p.add("vector", lambda e: e.tensor_scalar_max(n2_ap, n2_ap, 1e-12),
              reads=n2_k, writes=n2_k)
        p.add("vector", lambda e: e.reciprocal(n2_ap, n2_ap),
              reads=n2_k, writes=n2_k)
        bps_ap, bps_k = c.P1.view(0, (128, TLOC), F32)
        for nb in range(NB):
            sl = slice(512 * nb, 512 * (nb + 1))
            p.add("tensor", lambda e, sl=sl: e.matmul(
                bps_ap[:, sl], c.hs2T[:], n2_ap[:, sl], start=True, stop=True),
                reads=n2_k + ["hs2T"], writes=bps_k)
        p.add("vector", lambda e: e.tensor_mul(scr_ap, scr_ap, bps_ap),
              reads=scr_k + bps_k, writes=scr_k)
        p.dma("sync", outs["o_kk"][128 * j:128 * (j + 1), :], scr_ap,
              reads=scr_k, writes=[f"okk{j}"], semkey="o_scr")
        p.add("vector", lambda e: e.tensor_mul(sc2_ap, scr_ap, aj_ap),
              reads=scr_k + aj_k, writes=sc2_k)
        p.dma("sync", outs["o_b"][128 * j:128 * (j + 1), :], sc2_ap,
              reads=sc2_k, writes=[f"ob{j}"], semkey="o_sc2")
        p.add("vector", lambda e: e.tensor_mul(sc2_ap, kch_ap, aj_ap),
              reads=kch_k + aj_k, writes=sc2_k)
        p.add("vector", lambda e, j=j: e.tensor_scalar_mul(
            sc2_ap, sc2_ap, c.cn["wb_k_a"][:, j:j + 1]),
            reads=sc2_k + ["wb_k_a"], writes=sc2_k)
        p.add("vector", lambda e, j=j: e.scalar_tensor_tensor(
            sc2_ap, kch_ap, omka_ap[:, j:j + 1], sc2_ap, ALU.mult, ALU.add),
            reads=kch_k + omka_k + sc2_k, writes=sc2_k)
        p.dma("sync", outs["o_k2"][128 * j:128 * (j + 1), :], sc2_ap,
              reads=sc2_k, writes=[f"ok2{j}"], semkey="o_sc2")

    mk_mix("wb_x_g")
    psg_ap, psg_k = c.P0.view(0, (128, TLOC), F32)
    for i in range(NCH):
        for nb in range(NB):
            sl = slice(512 * nb, 512 * (nb + 1))
            p.add("tensor", lambda e, i=i, sl=sl: e.matmul(
                psg_ap[:, sl], c.sb_g1[:, 128 * i:128 * (i + 1)],
                M_aps[i][:, sl], start=(i == 0), stop=(i == NCH - 1)),
                reads=["sb_g1"] + M_keys[i], writes=psg_k)
    p.add("scalar", lambda e: e.activation(kch_ap, psg_ap, AF.Sigmoid),
          reads=psg_k, writes=kch_k)
    for j in range(NCH):
        ps_ap, ps_k = (c.P1 if j % 2 == 0 else c.P0).view(0, (128, TLOC), F32)
        for nb in range(NB):
            sl = slice(512 * nb, 512 * (nb + 1))
            p.add("tensor", lambda e, j=j, sl=sl, ps=ps_ap: e.matmul(
                ps[:, sl], c.sb_g2[:, 128 * j:128 * (j + 1)], kch_ap[:, sl],
                start=True, stop=True),
                reads=["sb_g2"] + kch_k, writes=ps_k)
        p.add("vector", lambda e, ps=ps_ap: e.tensor_copy(scr_ap, ps),
              reads=ps_k, writes=scr_k)
        p.dma("sync", outs["o_g"][128 * j:128 * (j + 1), :], scr_ap,
              reads=scr_k, writes=[f"og{j}"], semkey="o_scr")


def emit_s7b(c):
    p = c.prog
    nc = c.nc
    L = 64
    d_in = c.dram["wkv_in"]
    d_y = nc.dram_tensor("wkv_y", [4 * 8192, 64], BF16,
                         kind="ExternalOutput").ap()

    IN, INk = [], []
    for t6 in range(6):
        a, k = c.S.view(t6 * 4 * KB, (128, TLOC), BF16)
        IN.append(a)
        INk.append(k)
    ybuf, ybuf_k = c.S.view(24 * KB, (128, TLOC), BF16)

    slot_n = [0]
    def tile(dtype=BF16, width=64):
        off = 28 * KB + slot_n[0] * 256
        slot_n[0] += (width * (4 if dtype == F32 else 2) + 255) // 256
        return c.S.view(off, (128, width), dtype)

    cs_ap, cs_k = tile(F32)
    W_ap, W_k = tile(F32)
    iW_ap, iW_k = tile(F32)
    Wm1_ap, Wm1_k = tile(F32)
    en_ap, en_k = tile(F32)
    at_ap, at_k = tile()
    bt_ap, bt_k = tile()
    kt_ap, kt_k = tile()
    rt_ap, rt_k = tile()
    Atm_ap, Atm_k = tile()
    Btm_ap, Btm_k = tile()
    Ktm_ap, Ktm_k = tile()
    Vtm_ap, Vtm_k = tile()
    Gs_ap, Gs_k = tile()
    GsT_ap, GsT_k = tile()
    HsT_ap, HsT_k = tile()
    MBT_ap, MBT_k = tile()
    MKT_ap, MKT_k = tile()
    Nb = [tile(), tile()]
    NTb = [tile(), tile()]
    Pb = [tile(), tile()]
    PTb = [tile(), tile()]
    hv_ap, hv_k = tile()
    Z_ap, Z_k = tile()
    Ab_ap, Ab_k = tile()
    QcT_ap, QcT_k = tile()
    Cc_ap, Cc_k = tile()
    Ec_ap, Ec_k = tile()
    GcTs_ap, GcTs_k = tile()
    zz_ap, zz_k = tile()
    st_ap, st_k = tile()

    p.add("vector", lambda e: e.memset(zz_ap, 0.0), writes=zz_k)

    def pairmm(ps_ap, ps_k, lhs, lhs_k, rhs, rhs_k, start=True, stop=True):
        for b0 in (0, 64):
            p.add("tensor", lambda e, b0=b0, ps=ps_ap, l=lhs, r=rhs,
                  st=start, sp=stop: e.matmul(
                      ps[b0:b0 + 64, :], l[b0:b0 + 64, :], r[b0:b0 + 64, :],
                      start=st, stop=sp),
                reads=lhs_k + rhs_k, writes=ps_k)

    def pairtr(ps_ap, ps_k, src, src_k):
        for b0 in (0, 64):
            p.add("tensor", lambda e, b0=b0, ps=ps_ap, s2=src: e.transpose(
                ps[b0:b0 + 64, 0:64], s2[b0:b0 + 64, :],
                c.identbf[b0:b0 + 64, b0:b0 + 64]),
                reads=src_k + ["identbf"], writes=ps_k)

    for pr in range(2):
        p.add("vector", lambda e: e.memset(st_ap, 0.0), writes=st_k)
        for g in range(4):
            gsl = slice(2048 * g, 2048 * (g + 1))
            for t6 in range(6):
                for sub in range(2):
                    u = 2 * pr + sub
                    p.dma("sync", IN[t6][64 * sub:64 * (sub + 1), :],
                          d_in[u * 384 + t6 * 64:u * 384 + (t6 + 1) * 64, gsl],
                          writes=INk[t6], semkey=f"win{t6}")
            for ci in range(32):
                csl = slice(64 * ci, 64 * (ci + 1))
                rS, rS_k = IN[0], INk[0]
                lwS, lw_k2 = IN[1], INk[1]
                k2S, k2_k2 = IN[2], INk[2]
                vS, v_k2 = IN[3], INk[3]
                kkS, kk_k2 = IN[4], INk[4]
                bS, b_k2 = IN[5], INk[5]
                p.add("vector", lambda e, csl=csl: e.tensor_tensor_scan(
                    cs_ap, lwS[:, csl], zz_ap, 0.0, ALU.add, ALU.add),
                    reads=lw_k2 + zz_k, writes=cs_k, sticky=True)
                p.add("scalar", lambda e: e.activation(W_ap, cs_ap, AF.Exp),
                      reads=cs_k, writes=W_k)
                p.add("scalar", lambda e: e.activation(iW_ap, cs_ap, AF.Exp,
                                                       scale=-1.0),
                      reads=cs_k, writes=iW_k)
                p.add("scalar", lambda e, csl=csl: e.activation(
                    en_ap, lwS[:, csl], AF.Exp, scale=-1.0),
                    reads=lw_k2, writes=en_k)
                p.add("vector", lambda e: e.tensor_mul(Wm1_ap, W_ap, en_ap),
                      reads=W_k + en_k, writes=Wm1_k)
                p.add("vector", lambda e, csl=csl: e.tensor_mul(
                    at_ap, Wm1_ap, kkS[:, csl]),
                    reads=Wm1_k + kk_k2, writes=at_k)
                p.add("vector", lambda e: e.tensor_scalar_mul(at_ap, at_ap,
                                                              -1.0),
                      reads=at_k, writes=at_k)
                p.add("vector", lambda e, csl=csl: e.tensor_mul(
                    bt_ap, iW_ap, bS[:, csl]), reads=iW_k + b_k2, writes=bt_k)
                p.add("vector", lambda e, csl=csl: e.tensor_mul(
                    kt_ap, iW_ap, k2S[:, csl]), reads=iW_k + k2_k2, writes=kt_k)
                p.add("vector", lambda e, csl=csl: e.tensor_mul(
                    rt_ap, W_ap, rS[:, csl]), reads=W_k + rS_k, writes=rt_k)
                ps_t, ps_tk = c.P1.view(4096, (128, 64), BF16)
                for dst, dstk, src, srck in ((Atm_ap, Atm_k, at_ap, at_k),
                                             (Btm_ap, Btm_k, bt_ap, bt_k),
                                             (Ktm_ap, Ktm_k, kt_ap, kt_k)):
                    pairtr(ps_t, ps_tk, src, srck)
                    p.add("vector", lambda e, d2=dst, ps=ps_t: e.tensor_copy(
                        d2, ps), reads=ps_tk, writes=dstk)
                pairtr(ps_t, ps_tk, vS[:, csl].copy() if False else vS[:, csl],
                       v_k2)
                p.add("vector", lambda e, ps=ps_t: e.tensor_copy(Vtm_ap, ps),
                      reads=ps_tk, writes=Vtm_k)
                ps_a, ps_ak = c.P0.view(0, (128, 64), F32)
                ps_b, ps_bk = c.P0.view(2048, (128, 64), F32)
                pairmm(ps_a, ps_ak, at_ap, at_k, bt_ap, bt_k)
                p.add("vector", lambda e, ps=ps_a: e.tensor_tensor(
                    Gs_ap, ps, c.m_lts[:], ALU.mult),
                    reads=ps_ak + ["m_lts"], writes=Gs_k)
                pairmm(ps_b, ps_bk, bt_ap, bt_k, at_ap, at_k)
                p.add("vector", lambda e, ps=ps_b: e.tensor_tensor(
                    GsT_ap, ps, c.m_uts[:], ALU.mult),
                    reads=ps_bk + ["m_uts"], writes=GsT_k)
                pairmm(ps_a, ps_ak, kt_ap, kt_k, at_ap, at_k)
                p.add("vector", lambda e, ps=ps_a: e.tensor_tensor(
                    HsT_ap, ps, c.m_uts[:], ALU.mult),
                    reads=ps_ak + ["m_uts"], writes=HsT_k)
                pairmm(ps_b, ps_bk, bt_ap, bt_k, rt_ap, rt_k)
                p.add("vector", lambda e, ps=ps_b: e.tensor_tensor(
                    MBT_ap, ps, c.m_uti[:], ALU.mult),
                    reads=ps_bk + ["m_uti"], writes=MBT_k)
                pairmm(ps_a, ps_ak, kt_ap, kt_k, rt_ap, rt_k)
                p.add("vector", lambda e, ps=ps_a: e.tensor_tensor(
                    MKT_ap, ps, c.m_uti[:], ALU.mult),
                    reads=ps_ak + ["m_uti"], writes=MKT_k)
                p.add("vector", lambda e: e.tensor_add(Pb[0][0], Gs_ap,
                                                       c.I2[:]),
                      reads=Gs_k + ["I2"], writes=Pb[0][1])
                p.add("vector", lambda e: e.tensor_add(PTb[0][0], GsT_ap,
                                                       c.I2[:]),
                      reads=GsT_k + ["I2"], writes=PTb[0][1])
                p.add("vector", lambda e: e.tensor_copy(Nb[0][0], Gs_ap),
                      reads=Gs_k, writes=Nb[0][1])
                p.add("vector", lambda e: e.tensor_copy(NTb[0][0], GsT_ap),
                      reads=GsT_k, writes=NTb[0][1])
                for it in range(5):
                    cur, nxt = it % 2, 1 - it % 2
                    pairmm(ps_a, ps_ak, NTb[cur][0], NTb[cur][1],
                           Nb[cur][0], Nb[cur][1])
                    p.add("vector", lambda e, ps=ps_a, d2=Nb[nxt][0]:
                          e.tensor_copy(d2, ps),
                        reads=ps_ak, writes=Nb[nxt][1])
                    pairmm(ps_b, ps_bk, Nb[cur][0], Nb[cur][1],
                           NTb[cur][0], NTb[cur][1])
                    p.add("vector", lambda e, ps=ps_b, d2=NTb[nxt][0]:
                          e.tensor_copy(d2, ps),
                        reads=ps_bk, writes=NTb[nxt][1])
                    pairmm(ps_a, ps_ak, PTb[cur][0], PTb[cur][1],
                           Nb[nxt][0], Nb[nxt][1])
                    p.add("vector", lambda e, ps=ps_a, d2=Pb[nxt][0],
                          s2=Pb[cur][0]: e.tensor_add(d2, s2, ps),
                        reads=ps_ak + Pb[cur][1], writes=Pb[nxt][1])
                    pairmm(ps_b, ps_bk, Nb[nxt][0], Nb[nxt][1],
                           PTb[cur][0], PTb[cur][1])
                    p.add("vector", lambda e, ps=ps_b, d2=PTb[nxt][0],
                          s2=PTb[cur][0]: e.tensor_add(d2, s2, ps),
                        reads=ps_bk + PTb[cur][1], writes=PTb[nxt][1])
                TinvT, TinvT_k = PTb[1][0], PTb[1][1]
                pairmm(ps_a, ps_ak, HsT_ap, HsT_k, Vtm_ap, Vtm_k)
                p.add("vector", lambda e, ps=ps_a: e.tensor_copy(hv_ap, ps),
                      reads=ps_ak, writes=hv_k)
                pairmm(ps_b, ps_bk, TinvT, TinvT_k, hv_ap, hv_k)
                p.add("vector", lambda e, ps=ps_b: e.tensor_copy(Z_ap, ps),
                      reads=ps_bk, writes=Z_k)
                pairmm(ps_a, ps_ak, TinvT, TinvT_k, Atm_ap, Atm_k)
                p.add("vector", lambda e, ps=ps_a: e.tensor_copy(Ab_ap, ps),
                      reads=ps_ak, writes=Ab_k)
                pairmm(ps_b, ps_bk, Ab_ap, Ab_k, MBT_ap, MBT_k)
                p.add("vector", lambda e, ps=ps_b: e.tensor_add(
                    QcT_ap, ps, rt_ap), reads=ps_bk + rt_k, writes=QcT_k)
                pairmm(ps_a, ps_ak, MBT_ap, MBT_k, Z_ap, Z_k, start=True,
                       stop=False)
                pairmm(ps_a, ps_ak, MKT_ap, MKT_k, Vtm_ap, Vtm_k, start=False,
                       stop=True)
                p.add("vector", lambda e, ps=ps_a: e.tensor_copy(Cc_ap, ps),
                      reads=ps_ak, writes=Cc_k)
                pairmm(ps_b, ps_bk, Ab_ap, Ab_k, Btm_ap, Btm_k)
                p.add("vector", lambda e, ps=ps_b: e.tensor_add(
                    Ec_ap, ps, c.I2[:]), reads=ps_bk + ["I2"], writes=Ec_k)
                pairmm(ps_a, ps_ak, Btm_ap, Btm_k, Z_ap, Z_k, start=True,
                       stop=False)
                pairmm(ps_a, ps_ak, Ktm_ap, Ktm_k, Vtm_ap, Vtm_k, start=False,
                       stop=True)
                p.add("vector", lambda e, ps=ps_a: e.tensor_scalar_mul(
                    GcTs_ap, ps, W_ap[:, 63:64]),
                    reads=ps_ak + W_k, writes=GcTs_k)
                ps_y, ps_yk = c.P1.view(0, (128, 64), F32)
                ps_s2, ps_s2k = c.P1.view(2048, (128, 64), F32)
                pairmm(ps_y, ps_yk, QcT_ap, QcT_k, st_ap, st_k)
                pairmm(ps_s2, ps_s2k, Ec_ap, Ec_k, st_ap, st_k)
                p.add("vector", lambda e, ps=ps_y, csl=csl: e.tensor_add(
                    ybuf[:, csl], ps, Cc_ap),
                    reads=ps_yk + Cc_k, writes=ybuf_k)
                p.add("vector", lambda e, ps=ps_s2: e.scalar_tensor_tensor(
                    st_ap, ps, W_ap[:, 63:64], GcTs_ap, ALU.mult, ALU.add),
                    reads=ps_s2k + W_k + GcTs_k, writes=st_k)
            for sub in range(2):
                u = 2 * pr + sub
                p.dma("sync",
                      d_y[u * 8192 + 2048 * g:u * 8192 + 2048 * (g + 1), :]
                      .rearrange("(ci t) v -> t ci v", t=64),
                      ybuf[64 * sub:64 * (sub + 1), :]
                      .rearrange("t (ci v) -> t ci v", v=64),
                      reads=ybuf_k, writes=[f"yo{u}_{g}"], semkey="ywr")


def emit_s7c(c):
    p = c.prog
    EPSG = 0.00064
    yfin, yfink = [], []
    for j in range(NCH):
        a, k = c.S.view(j * 4 * KB, (128, TLOC), BF16)
        yfin.append(a)
        yfink.append(k)
    wcur_ap, wcur_k = c.S.view(32 * KB, (128, TLOC), BF16)
    t2_ap, t2_k = c.S.view(36 * KB, (128, TLOC), BF16)
    t3_ap, t3_k = c.S.view(40 * KB, (128, TLOC), BF16)
    t4_ap, t4_k = c.S.view(44 * KB, (128, TLOC), BF16)
    mu2_ap, mu2_k = c.S.view(48 * KB, (2, TLOC), F32)
    va2_ap, va2_k = c.S.view(56 * KB, (2, TLOC), F32)
    muB_ap, muB_k = c.S.view(64 * KB, (128, TLOC), F32)
    rsB_ap, rsB_k = c.S.view(72 * KB, (128, TLOC), F32)
    sB_ap, sB_k = c.S.view(80 * KB, (128, TLOC), F32)

    for j in range(NCH):
        p.dma("sync", wcur_ap, c.dram["wy"][128 * j:128 * (j + 1), :],
              writes=wcur_k, semkey="l_wy")
        p.dma("sync", t3_ap, c.dram["lr"][128 * j:128 * (j + 1), :],
              writes=t3_k, semkey="l_lr")
        p.dma("sync", t4_ap, c.dram["lk2"][128 * j:128 * (j + 1), :],
              writes=t4_k, semkey="l_lk2")
        nps_ap, nps_k = c.P0.view(0, (2, TLOC), F32)
        qps_ap, qps_k = c.P1.view(0, (2, TLOC), F32)
        for nb in range(NB):
            sl = slice(512 * nb, 512 * (nb + 1))
            p.add("tensor", lambda e, sl=sl: e.matmul(
                nps_ap[:, sl], c.hs2[:], wcur_ap[:, sl],
                start=True, stop=True),
                reads=wcur_k + ["hs2"], writes=nps_k)
        p.add("scalar", lambda e: e.activation(t2_ap, wcur_ap, AF.Square),
              reads=wcur_k, writes=t2_k)
        for nb in range(NB):
            sl = slice(512 * nb, 512 * (nb + 1))
            p.add("tensor", lambda e, sl=sl: e.matmul(
                qps_ap[:, sl], c.hs2[:], t2_ap[:, sl], start=True, stop=True),
                reads=t2_k + ["hs2"], writes=qps_k)
        p.add("vector", lambda e: e.tensor_scalar_mul(mu2_ap, nps_ap, 1.0 / 64),
              reads=nps_k, writes=mu2_k)
        p.add("vector", lambda e: e.tensor_scalar_mul(va2_ap, qps_ap, 1.0 / 64),
              reads=qps_k, writes=va2_k)
        p.add("vector", lambda e: e.tensor_mul(sB_ap[0:2, :], mu2_ap, mu2_ap),
              reads=mu2_k, writes=sB_k)
        p.add("vector", lambda e: e.tensor_sub(va2_ap, va2_ap, sB_ap[0:2, :]),
              reads=va2_k + sB_k, writes=va2_k)
        p.add("vector", lambda e: e.tensor_scalar_add(va2_ap, va2_ap, EPSG),
              reads=va2_k, writes=va2_k)
        p.add("scalar", lambda e: e.activation(va2_ap, va2_ap, AF.Sqrt),
              reads=va2_k, writes=va2_k)
        p.add("vector", lambda e: e.reciprocal(va2_ap, va2_ap),
              reads=va2_k, writes=va2_k)
        bps_ap, bps_k = c.P1.view(0, (128, TLOC), F32)
        for src_ap, src_k, dst_ap, dst_k in ((mu2_ap, mu2_k, muB_ap, muB_k),
                                             (va2_ap, va2_k, rsB_ap, rsB_k)):
            for nb in range(NB):
                sl = slice(512 * nb, 512 * (nb + 1))
                p.add("tensor", lambda e, sl=sl, s2=src_ap: e.matmul(
                    bps_ap[:, sl], c.hs2T[:], s2[:, sl], start=True, stop=True),
                    reads=src_k + ["hs2T"], writes=bps_k)
            p.add("vector", lambda e, d2=dst_ap, ps=bps_ap: e.tensor_copy(
                d2, ps), reads=bps_k, writes=dst_k)
        p.add("vector", lambda e: e.tensor_sub(wcur_ap, wcur_ap, muB_ap),
              reads=wcur_k + muB_k, writes=wcur_k)
        p.add("vector", lambda e: e.tensor_mul(wcur_ap, wcur_ap, rsB_ap),
              reads=wcur_k + rsB_k, writes=wcur_k)
        p.add("vector", lambda e, j=j: e.tensor_scalar(
            wcur_ap, wcur_ap, c.cn["wb_lnw"][:, j:j + 1],
            c.cn["wb_lnb"][:, j:j + 1], ALU.mult, ALU.add),
            reads=wcur_k + ["wb_lnw", "wb_lnb"], writes=wcur_k)
        p.add("vector", lambda e: e.tensor_mul(t2_ap, t3_ap, t4_ap),
              reads=t3_k + t4_k, writes=t2_k)
        for nb in range(NB):
            sl = slice(512 * nb, 512 * (nb + 1))
            p.add("tensor", lambda e, j=j, sl=sl: e.matmul(
                nps_ap[:, sl], c.rksel_sb[:, 2 * j:2 * (j + 1)], t2_ap[:, sl],
                start=True, stop=True),
                reads=t2_k + ["rksel"], writes=nps_k)
        p.add("vector", lambda e: e.tensor_copy(mu2_ap, nps_ap),
              reads=nps_k, writes=mu2_k)
        for nb in range(NB):
            sl = slice(512 * nb, 512 * (nb + 1))
            p.add("tensor", lambda e, sl=sl: e.matmul(
                bps_ap[:, sl], c.hs2T[:], mu2_ap[:, sl], start=True, stop=True),
                reads=mu2_k + ["hs2T"], writes=bps_k)
        p.add("vector", lambda e, ps=bps_ap: e.tensor_copy(sB_ap, ps),
              reads=bps_k, writes=sB_k)
        p.dma("sync", t3_ap, c.dram["lv"][128 * j:128 * (j + 1), :],
              writes=t3_k, semkey="l_lr")
        p.add("vector", lambda e: e.tensor_mul(t4_ap, sB_ap, t3_ap),
              reads=sB_k + t3_k, writes=t4_k)
        p.add("vector", lambda e: e.tensor_add(wcur_ap, wcur_ap, t4_ap),
              reads=wcur_k + t4_k, writes=wcur_k)
        p.dma("sync", t3_ap, c.dram["lg"][128 * j:128 * (j + 1), :],
              writes=t3_k, semkey="l_lr")
        p.add("vector", lambda e, j=j: e.tensor_mul(yfin[j], wcur_ap, t3_ap),
              reads=wcur_k + t3_k, writes=yfink[j])

    for j in range(NCH):
        w_ap, w_k = load_wblock(c, "wb_wo", j, (j % 2) * 2 * KB, D)
        ps_ap, ps_k = (c.P0 if j % 2 == 0 else c.P1).view(0, (128, TLOC), F32)
        for i in range(NCH):
            for nb in range(NB):
                sl = slice(512 * nb, 512 * (nb + 1))
                p.add("tensor", lambda e, i=i, sl=sl, w=w_ap, ps=ps_ap,
                      st=(i == 0), sp=(i == NCH - 1): e.matmul(
                          ps[:, sl], w[:, 128 * i:128 * (i + 1)],
                          yfin[i][:, sl], start=st, stop=sp),
                    reads=w_k + yfink[i], writes=ps_k)
        p.add("vector", lambda e, j=j, ps=ps_ap: e.scalar_tensor_tensor(
            c.x[j][:], ps, 1.0, c.x[j][:], ALU.mult, ALU.add),
            reads=ps_k + [f"x{j}"], writes=[f"x{j}"])


EMITTERS = {
    "s1": emit_l0_attn_a,
    "s7a": emit_s7a,
    "s7b": emit_s7b,
    "s7c": emit_s7c,
    "s3": emit_l0_attn_b,
    "s5": emit_l1_attn_a,
    "s2": emit_l0_ffn,
    "s4": emit_l0_ffn,
    "s6": emit_l1_ffn,
    "s8": emit_l1_ffn,
}


def build_program(sublayers, wshapes, tap=None):
    nc = bass.Bass("TRN2", target_bir_lowering=False, num_devices=NCORES)
    c = Ctx()
    c.nc = nc
    c.prog = Prog(nc)
    p = c.prog

    c.dram = {}
    for name in sorted(wshapes):
        sh, dtname = wshapes[name]
        dt_ = BF16 if dtname == "bfloat16" else F32
        c.dram[name] = nc.dram_tensor(name, list(sh), dt_,
                                      kind="ExternalInput").ap()
    d_xfm = nc.dram_tensor("xfm", [D, TLOC], F32, kind="ExternalInput").ap()
    d_xfm_out = nc.dram_tensor("xfm_out", [D, TLOC], F32,
                               kind="ExternalOutput").ap()

    ctx = ExitStack()
    with ctx:
        c.x = [ctx.enter_context(nc.sbuf_tensor(f"x{j}", [128, TLOC], F32))
               for j in range(NCH)]
        c.y = [ctx.enter_context(nc.sbuf_tensor(f"y{j}", [128, TLOC], BF16))
               for j in range(NCH)]
        W_NEED = {"s1": 8, "s2": 32, "s3": 24, "s4": 32, "s5": 20,
                  "s6": 16, "s7a": 4, "s7b": 4, "s7c": 4, "s8": 16}
        S_NEED = {"s1": 74, "s2": 74, "s3": 86, "s4": 74, "s5": 86,
                  "s6": 74, "s7a": 94, "s7b": 94, "s7c": 94, "s8": 74}
        c.S = Pool(nc, ctx, "S", max(S_NEED[sl] for sl in sublayers) * KB)
        c.W = Pool(nc, ctx, "W",
                   max(W_NEED[sl] for sl in sublayers) * KB)
        ps0 = ctx.enter_context(nc.psum_tensor("ps0", [128, 2048], F32))
        ps1 = ctx.enter_context(nc.psum_tensor("ps1", [128, 2048], F32))
        c.P0 = PsPool(ps0, "P0")
        c.P1 = PsPool(ps1, "P1")

        c.ones_col = ctx.enter_context(nc.sbuf_tensor("ones_col", [128, 1], F32))
        c.ones_row = ctx.enter_context(nc.sbuf_tensor("ones_row", [1, 128], F32))
        c.ones_row_bf = ctx.enter_context(
            nc.sbuf_tensor("ones_row_bf", [1, 128], BF16))
        d_ones = nc.inline_tensor(np.ones((128, 1), np.float32), "d_ones").ap()
        d_ones_bf = nc.inline_tensor(
            np.ones((1, 128), ml_dtypes.bfloat16), "d_ones_bf").ap()
        p.dma("sync", c.ones_col[:], d_ones, writes=["ones_col"])
        p.dma("sync", c.ones_row[:], d_ones.rearrange("p o -> o p"),
              writes=["ones_row"])
        p.dma("sync", c.ones_row_bf[:], d_ones_bf, writes=["ones_row_bf"])

        if any(sl in ("s2", "s4") for sl in sublayers):
            c.rt_gs = ctx.enter_context(nc.sbuf_tensor("rt_gs", [128, 8], F32))
            c.rt_m8 = ctx.enter_context(nc.sbuf_tensor("rt_m8", [128, 8], F32))
            c.rt_gm = ctx.enter_context(nc.sbuf_tensor("rt_gm", [128, 8], F32))
            c.rt_msk = ctx.enter_context(nc.sbuf_tensor("rt_msk", [128, 64], F32))
            c.ident = ctx.enter_context(nc.sbuf_tensor("ident", [128, 128], F32))
            d_eye = nc.inline_tensor(np.eye(128, dtype=np.float32), "d_eye").ap()
            p.dma("sync", c.ident[:], d_eye, writes=["ident"])
            c.cn2 = {}
            t = ctx.enter_context(nc.sbuf_tensor("sb_m_gate", [128, 8 * 64], BF16))
            p.dma("gpsimd", t[:].rearrange("p (i m) -> p i m", i=8),
                  c.dram["m_gate"].rearrange("(i p) m -> p i m", p=128),
                  writes=["m_gate"])
            c.cn2 = {"m_gate": t}
            gbr = ctx.enter_context(nc.sbuf_tensor("sb_gbias", [1, 64], F32))
            p.dma("sync", gbr[:], c.dram["m_gbias"], writes=["gbias_r"])
            c.gb_bcast = ctx.enter_context(
                nc.sbuf_tensor("gb_bcast", [128, 64], F32))
            gbp, gbp_k = c.P0.view(0, (128, 64), F32)
            p.add("tensor", lambda e: e.matmul(gbp, c.ones_row[:], gbr[:],
                                               start=True, stop=True),
                  reads=["gbias_r", "ones_row"], writes=gbp_k)
            p.add("vector", lambda e: e.tensor_copy(c.gb_bcast[:], gbp),
                  reads=gbp_k, writes=["gb_bcast"])
            h_cr = nc.dram_tensor("combrows", [64, TLOC], BF16, kind="Internal")
            c.dram_combrows = h_cr.ap()
            c.dram_combrows_r = h_cr.ap()

        if "s3" in sublayers:
            c.permswap = ctx.enter_context(
                nc.sbuf_tensor("permswap", [128, 128], BF16))
            pm = np.zeros((128, 128), ml_dtypes.bfloat16)
            for blk in range(2):
                for r in range(32):
                    pm[blk * 64 + 32 + r, blk * 64 + r] = 1
                    pm[blk * 64 + r, blk * 64 + 32 + r] = 1
            d_pm = nc.inline_tensor(pm, "d_pm").ap()
            p.dma("sync", c.permswap[:], d_pm, writes=["permswap"])
            c.identbf = ctx.enter_context(
                nc.sbuf_tensor("identbf", [128, 128], BF16))
            d_eyebf = nc.inline_tensor(
                np.eye(128).astype(ml_dtypes.bfloat16), "d_eyebf").ap()
            p.dma("sync", c.identbf[:], d_eyebf, writes=["identbf"])
            c.ropec, c.ropec_k = c.S.view(44 * KB, (128, TLOC), BF16)
            c.ropes, c.ropes_k = c.S.view(48 * KB, (128, TLOC), BF16)
            p.dma("gpsimd", c.ropec, c.dram["rope_cos"], writes=c.ropec_k)
            p.dma("gpsimd", c.ropes, c.dram["rope_sin"], writes=c.ropes_k)
            c.cc_k_in = nc.dram_tensor("cc_k_in", [D, TLOC], BF16,
                                       kind="Internal").ap()
            c.cc_k_out = nc.dram_tensor("cc_k_out", [4 * D, TLOC], BF16,
                                        kind="Internal").ap()
            c.ccvig = nc.dram_tensor("ccvig", [D, TLOC], BF16,
                                       kind="Internal").ap()
            c.ccvog = nc.dram_tensor("ccvog", [4 * D, TLOC], BF16,
                                        kind="Internal").ap()
            for j in range(NCH):
                ap_, k_ = c.W.view(8 * KB + j * 2 * KB, (128, 8 * 128), BF16)
                src = c.dram["ab_o"][j].rearrange("(i p) m -> p i m", p=128)
                p.dma("gpsimd", ap_.rearrange("p (i m) -> p i m", i=8), src,
                      writes=k_)

        if "s5" in sublayers:
            if not hasattr(c, "identbf"):
                c.identbf = ctx.enter_context(
                    nc.sbuf_tensor("identbf", [128, 128], BF16))
                d_eyebf2 = nc.inline_tensor(
                    np.eye(128).astype(ml_dtypes.bfloat16), "d_eyebf2").ap()
                p.dma("sync", c.identbf[:], d_eyebf2, writes=["identbf"])
            c.perm2 = ctx.enter_context(
                nc.sbuf_tensor("perm2", [128, 128], BF16))
            pm2 = np.zeros((128, 128), ml_dtypes.bfloat16)
            for m in range(128):
                pm2[m ^ 1, m] = 1
            d_pm2 = nc.inline_tensor(pm2, "d_pm2").ap()
            p.dma("sync", c.perm2[:], d_pm2, writes=["perm2"])
            c.wamask = ctx.enter_context(
                nc.sbuf_tensor("wamask", [128, 8], F32))
            p.dma("sync", c.wamask[:], c.dram["wa_mask"], writes=["wamask"])
            c.cc_a_in = nc.dram_tensor("cc_a_in", [2 * 8192, 128], BF16,
                                       kind="Internal").ap()
            c.cc_a_out = nc.dram_tensor("cc_a_out", [8 * 8192, 128], BF16,
                                        kind="Internal").ap()

        if "s7a" in sublayers:
            def ld_const(name, src_name, shape, rearr=None, dt=BF16):
                t = ctx.enter_context(nc.sbuf_tensor(name, shape, dt))
                src = c.dram[src_name]
                if rearr:
                    p.dma("gpsimd", t[:].rearrange(rearr[0], i=8),
                          src.rearrange(rearr[1], p=128), writes=[name])
                else:
                    p.dma("gpsimd", t[:], src, writes=[name])
                return t
            c.sb_a1 = ld_const("sb_a1", "wb_a1", [128, 8 * 64],
                               ("p (i m) -> p i m", "(i p) m -> p i m"))
            c.sb_w1 = ld_const("sb_w1", "wb_w1", [128, 8 * 64],
                               ("p (i m) -> p i m", "(i p) m -> p i m"))
            c.sb_g1 = ld_const("sb_g1", "wb_g1", [128, 8 * 128],
                               ("p (i m) -> p i m", "(i p) m -> p i m"))
            c.sb_a2 = ld_const("sb_a2", "wb_a2", [64, 1024])
            c.sb_w2 = ld_const("sb_w2", "wb_w2", [64, 1024])
            c.sb_g2 = ld_const("sb_g2", "wb_g2", [128, 1024])
            hs2 = np.zeros((128, 2), ml_dtypes.bfloat16)
            hs2[0:64, 0] = 1
            hs2[64:128, 1] = 1
            c.hs2 = ctx.enter_context(nc.sbuf_tensor("hs2", [128, 2], BF16))
            p.dma("sync", c.hs2[:], nc.inline_tensor(hs2, "d_hs2").ap(),
                  writes=["hs2"])
            hs2t = np.zeros((2, 128), np.float32)
            hs2t[0, 0:64] = 1
            hs2t[1, 64:128] = 1
            c.hs2T = ctx.enter_context(nc.sbuf_tensor("hs2T", [2, 128], F32))
            p.dma("sync", c.hs2T[:], nc.inline_tensor(hs2t, "d_hs2t").ap(),
                  writes=["hs2T"])

        if "s7c" in sublayers:
            if not hasattr(c, "hs2"):
                hs2c = np.zeros((128, 2), ml_dtypes.bfloat16)
                hs2c[0:64, 0] = 1
                hs2c[64:128, 1] = 1
                c.hs2 = ctx.enter_context(nc.sbuf_tensor("hs2", [128, 2], BF16))
                p.dma("sync", c.hs2[:],
                      nc.inline_tensor(hs2c, "d_hs2c").ap(), writes=["hs2"])
                hs2tc = np.zeros((2, 128), np.float32)
                hs2tc[0, 0:64] = 1
                hs2tc[1, 64:128] = 1
                c.hs2T = ctx.enter_context(nc.sbuf_tensor("hs2T", [2, 128], F32))
                p.dma("sync", c.hs2T[:],
                      nc.inline_tensor(hs2tc, "d_hs2tc").ap(), writes=["hs2T"])
            c.rksel_sb = ctx.enter_context(
                nc.sbuf_tensor("rksel_sb", [128, 16], BF16))
            p.dma("gpsimd", c.rksel_sb[:].rearrange("p (j m) -> p j m", j=8),
                  c.dram["wb_rksel"].rearrange("j p m -> p j m"),
                  writes=["rksel"])
        if "s7b" in sublayers:
            if not hasattr(c, "identbf"):
                c.identbf = ctx.enter_context(
                    nc.sbuf_tensor("identbf", [128, 128], BF16))
                d_eyebf3 = nc.inline_tensor(
                    np.eye(128).astype(ml_dtypes.bfloat16), "d_eyebf3").ap()
                p.dma("sync", c.identbf[:], d_eyebf3, writes=["identbf"])
            tri = np.tril(np.ones((64, 64), np.float32), -1)
            for nm, m in (("m_lts", np.tril(np.ones((64, 64)), -1)),
                          ("m_uts", np.triu(np.ones((64, 64)), 1)),
                          ("m_uti", np.triu(np.ones((64, 64)), 0)),
                          ("I2", np.eye(64))):
                t = ctx.enter_context(nc.sbuf_tensor(nm, [128, 64], F32))
                dd = nc.inline_tensor(
                    np.tile(m.astype(np.float32), (2, 1)), "d_" + nm).ap()
                p.dma("sync", t[:], dd, writes=[nm])
                setattr(c, nm, t)

        c.cn = {}
        for sl in sublayers:
            for name in SMALL_CONSTS.get(sl, []):
                if name in c.cn:
                    continue
                t = ctx.enter_context(
                    nc.sbuf_tensor("sb_" + name, list(wshapes[name][0]), F32))
                p.dma("sync", t[:], c.dram[name], writes=[name])
                c.cn[name] = t

        for j in range(NCH):
            p.dma("sync", c.x[j][:], d_xfm[128 * j:128 * (j + 1), :],
                  writes=[f"x{j}"])

        d_tap = None
        if tap:
            d_tap = nc.dram_tensor("tap", [D, TLOC], BF16, kind="ExternalOutput").ap()
        c.tap = tap
        c.d_tap = d_tap
        for sl in sublayers:
            emit_ln(c)
            if tap == "y":
                for j in range(NCH):
                    p.dma("scalar", d_tap[128*j:128*(j+1), :], c.y[j][:],
                          reads=[f"y{j}"], writes=[f"tap{j}"])
            EMITTERS[sl](c)
            if tap == "v":
                for j in range(NCH):
                    ap, k = c.S.view(24 * KB + j * 4 * KB, (128, TLOC), BF16)
                    p.dma("scalar", d_tap[128*j:128*(j+1), :], ap,
                          reads=k, writes=[f"tap{j}"])

        for j in range(NCH):
            p.dma("sync", d_xfm_out[128 * j:128 * (j + 1), :], c.x[j][:],
                  reads=[f"x{j}"], writes=[f"xo{j}"], semkey="xout")
        p.fence("sync")

        block = ctx.enter_context(nc.Block())
        p.build(block, ctx)
    return nc


_BUILD_CACHE = {}


def _get_program(sublayers, wshapes, tap=None):
    key = (tuple(sublayers), tap)
    if key not in _BUILD_CACHE:
        _BUILD_CACHE[key] = build_program(sublayers, wshapes, tap)
    return _BUILD_CACHE[key]


def rope_tables_s3(core, x_full=None):
    t0 = 2048 * (core % 4)
    theta = 1.0 / 10000.0 ** (np.arange(32, dtype=np.float64) / 32)
    pos = np.arange(t0, t0 + TLOC, dtype=np.float64)
    ang = pos[None, :] * theta[:, None]
    cos = np.cos(ang).astype(np.float32)
    sin = np.sin(ang).astype(np.float32)
    cos_t = np.tile(cos, (4, 1))
    sin_t = np.tile(np.concatenate([-sin, sin], 0), (2, 1))
    return {"rope_cos": np.ascontiguousarray(cos_t),
            "rope_sin": np.ascontiguousarray(sin_t)}


def wa_mask_s5(core, x_full=None):
    m = np.zeros((128, 8), np.float32)
    r = core % 4
    m[:, 2 * r] = 1.0
    m[:, 2 * r + 1] = 1.0
    return {"wa_mask": m}


def xprev_s7a(core, x_full):
    b, t0 = core // 4, 2048 * (core % 4)
    if t0 == 0:
        col = np.zeros((D, 1), np.float32)
    else:
        col = np.ascontiguousarray(x_full[b, t0 - 1, :].reshape(D, 1))
    return {"xprev": col}


EXTRA_PER_CORE = {"s3": rope_tables_s3, "s5": wa_mask_s5, "s7a": xprev_s7a}


def run_launch(sublayers, x_full, iw, tap=None, taps_out=None, outputs=(),
               extra_in=None):
    wshapes = {k: (iw[k].shape, str(iw[k].dtype))
               for sl in sublayers for k in WEIGHT_KEYS[sl]}
    extras = [dict() for _ in range(NCORES)]
    for sl in sublayers:
        fn = EXTRA_PER_CORE.get(sl)
        if fn:
            for cc_ in range(NCORES):
                extras[cc_].update(fn(cc_, x_full))
    if extra_in:
        for cc_ in range(NCORES):
            extras[cc_].update(extra_in[cc_])
    for k, v in extras[0].items():
        wshapes[k] = (v.shape, str(v.dtype))
    nc = _get_program(sublayers, wshapes, tap)
    in_maps = []
    for cc_ in range(NCORES):
        b, t0 = cc_ // 4, 2048 * (cc_ % 4)
        m = {"xfm": np.ascontiguousarray(x_full[b, t0:t0 + TLOC, :].T)}
        for k in wshapes:
            if k in extras[cc_]:
                m[k] = extras[cc_][k]
            else:
                m[k] = iw[k]
        in_maps.append(m)
    res = run_bass_kernel_spmd(nc, in_maps, core_ids=list(range(NCORES)))
    out = np.empty((2, 8192, D), np.float32)
    extra_out = [dict() for _ in range(NCORES)]
    for cc_ in range(NCORES):
        b, t0 = cc_ // 4, 2048 * (cc_ % 4)
        out[b, t0:t0 + TLOC, :] = res.results[cc_]["xfm_out"].T
        if tap and taps_out is not None:
            taps_out.append(res.results[cc_]["tap"])
        for nm in outputs:
            extra_out[cc_][nm] = res.results[cc_][nm]
    if outputs:
        return out, extra_out
    return out


def run_s7(x_full, iw):
    names = ["o_r", "o_lw", "o_k2", "o_v", "o_kk", "o_b", "o_g"]
    x_out, ext = run_launch(["s7a"], x_full, iw, outputs=names)
    full = {}
    for nm in names:
        f = np.empty((2, 8192, D), ml_dtypes.bfloat16)
        for cc_ in range(NCORES):
            b, t0 = cc_ // 4, 2048 * (cc_ % 4)
            f[b, t0:t0 + TLOC, :] = ext[cc_][nm].T
        full[nm] = f
    scan_names = ["o_r", "o_lw", "o_k2", "o_v", "o_kk", "o_b"]
    extra_b = []
    for cc_ in range(NCORES):
        b = cc_ // 4
        arr = np.empty((4 * 6 * 64, 8192), ml_dtypes.bfloat16)
        for ui in range(4):
            h = 4 * (cc_ % 4) + ui
            for t6, nm in enumerate(scan_names):
                arr[ui * 384 + t6 * 64:ui * 384 + (t6 + 1) * 64, :] = \
                    full[nm][b, :, 64 * h:64 * (h + 1)].T
        extra_b.append({"wkv_in": arr})
    _, ext_b = run_launch(["s7b"], x_full, iw, outputs=["wkv_y"],
                          extra_in=extra_b)
    y_full = np.empty((2, 8192, D), ml_dtypes.bfloat16)
    for cc_ in range(NCORES):
        b = cc_ // 4
        wy = ext_b[cc_]["wkv_y"]
        for ui in range(4):
            h = 4 * (cc_ % 4) + ui
            y_full[b, :, 64 * h:64 * (h + 1)] = wy[ui * 8192:(ui + 1) * 8192, :]
    extra_c = []
    for cc_ in range(NCORES):
        b, t0 = cc_ // 4, 2048 * (cc_ % 4)
        sl = lambda f: np.ascontiguousarray(f[b, t0:t0 + TLOC, :].T)
        extra_c.append({"wy": sl(y_full), "lr": sl(full["o_r"]),
                        "lk2": sl(full["o_k2"]), "lv": sl(full["o_v"]),
                        "lg": sl(full["o_g"])})
    return run_launch(["s7c"], x_full, iw, extra_in=extra_c)


LAUNCH_PLAN = [["s1"], ["s2"], ["s3"], ["s4"], ["s5"], ["s6"], "S7", ["s8"]]


def kernel(**inputs):
    iw = prep_weights(inputs)
    x = np.asarray(inputs["L0_attn_a_x"]).astype(np.float32)
    for group in LAUNCH_PLAN:
        if group == "S7":
            x = run_s7(x, iw)
        else:
            x = run_launch(group, x, iw)
    return x
